# version 03
# baseline (speedup 1.0000x reference)
import math
from contextlib import ExitStack

import numpy as np
import concourse.bass as bass
import concourse.mybir as mybir

F32 = mybir.dt.float32
BF16 = mybir.dt.bfloat16
I32 = mybir.dt.int32
AF = mybir.ActivationFunctionType
ALU = mybir.AluOpType
AX = mybir.AxisListType

D = 4096
DC = 32
CW = 3072
NHEAD, DH = 16, 64
WQK = 1024


class Cfg:
    def __init__(self, T=2048, n_ic=86, debug=(),
                 stages=("s0", "s1a", "s1b", "s2", "s3", "s3b", "s4a", "s4b", "s5")):
        self.T = T
        self.TC = T + 2
        self.n_ic = n_ic
        self.I = n_ic * 128
        self.IBC = 2
        self.debug = set(debug)
        self.stages = set(stages)
        n_ta = math.ceil(self.TC / 410)
        w = math.ceil(self.TC / n_ta)
        self.ta = []
        c = 0
        while c < self.TC:
            ww = min(w, self.TC - c)
            self.ta.append((c, ww))
            c += ww
        self.s0 = []
        c = 0
        while c + 128 < self.TC:
            self.s0.append(c)
            c += 128
        self.s0.append(self.TC - 128)
        assert T % 128 == 0 and T >= 256


def ceil_div(a, b):
    return -(-a // b)


class Ch:
    live = []

    def __init__(self, nc, name):
        self.sem = nc.alloc_semaphore(name)
        self.n = 0
        Ch.live.append(self)

    def inc(self, instr, k=1):
        instr.then_inc(self.sem, k)
        self.n += k


def build_program(cfg: Cfg):
    nc = bass.Bass("TRN2", target_bir_lowering=False)
    Ch.live = []

    def sem_mark():
        return len(Ch.live)

    def sem_free(mark):
        nc.clear_and_free_semaphores([c.sem for c in Ch.live[mark:]])
        del Ch.live[mark:]
    T, TC, n_ic, I_ = cfg.T, cfg.TC, cfg.n_ic, cfg.I
    on = lambda s: s in cfg.stages

    x_own = nc.dram_tensor("x_own", [T, D], F32, kind="ExternalInput").ap()
    x_hl = nc.dram_tensor("x_hl", [1, D], F32, kind="ExternalInput").ap()
    x_hr = nc.dram_tensor("x_hr", [1, D], F32, kind="ExternalInput").ap()
    w_conv = nc.dram_tensor("w_conv", [3, D, CW], F32, kind="ExternalInput").ap()
    b_conv = nc.dram_tensor("b_conv", [CW], F32, kind="ExternalInput").ap()
    w_q = nc.dram_tensor("w_q", [WQK, NHEAD * DH], F32, kind="ExternalInput").ap()
    b_q = nc.dram_tensor("b_q", [NHEAD * DH], F32, kind="ExternalInput").ap()
    w_k = nc.dram_tensor("w_k", [WQK, NHEAD * DH], F32, kind="ExternalInput").ap()
    b_k = nc.dram_tensor("b_k", [NHEAD * DH], F32, kind="ExternalInput").ap()
    w_v = nc.dram_tensor("w_v", [WQK, NHEAD * DH], F32, kind="ExternalInput").ap()
    b_v = nc.dram_tensor("b_v", [NHEAD * DH], F32, kind="ExternalInput").ap()
    w_o = nc.dram_tensor("w_o", [NHEAD * DH, WQK], F32, kind="ExternalInput").ap()
    b_o = nc.dram_tensor("b_o", [WQK], F32, kind="ExternalInput").ap()
    w_p = nc.dram_tensor("w_p", [WQK, D], F32, kind="ExternalInput").ap()
    b_p = nc.dram_tensor("b_p", [D], F32, kind="ExternalInput").ap()
    gq = nc.dram_tensor("gq", [I_, D // 2], I32, kind="ExternalInput").ap()
    uq = nc.dram_tensor("uq", [I_, D // 2], I32, kind="ExternalInput").ap()
    dq = nc.dram_tensor("dq", [D, I_ // 2], I32, kind="ExternalInput").ap()
    gsc = nc.dram_tensor("gsc", [I_, D // 64], F32, kind="ExternalInput").ap()
    gzr = nc.dram_tensor("gzr", [I_, D // 64], F32, kind="ExternalInput").ap()
    usc = nc.dram_tensor("usc", [I_, D // 64], F32, kind="ExternalInput").ap()
    uzr = nc.dram_tensor("uzr", [I_, D // 64], F32, kind="ExternalInput").ap()
    dsc = nc.dram_tensor("dsc", [D, I_ // 64], F32, kind="ExternalInput").ap()
    dzr = nc.dram_tensor("dzr", [D, I_ // 64], F32, kind="ExternalInput").ap()

    out = nc.dram_tensor("out", [T, D], F32, kind="ExternalOutput").ap()

    x_fm = nc.dram_tensor("x_fm", [D, TC], F32).ap()
    stage_d = nc.dram_tensor("stage_d", [D, I_], BF16).ap()
    NBLK_A = n_ic // 2
    gT_flat = nc.dram_tensor("gT_flat", [NBLK_A * 128 * DC * 256], BF16).ap()
    uT_flat = nc.dram_tensor("uT_flat", [NBLK_A * 128 * DC * 256], BF16).ap()
    h_flat = nc.dram_tensor("h_flat", [5 * 128 * n_ic * 512], BF16).ap()
    y_flat = nc.dram_tensor("y_flat", [5 * 128 * DC * 512], BF16).ap()

    def gT_view(t, b_):
        base = b_ * 128 * DC * 256
        return t[base:base + 128 * DC * 256].rearrange("(p c i) -> p c i", p=128, c=DC)

    def h_view(tt, w):
        base = tt * 128 * n_ic * 512
        return h_flat[base:base + 128 * n_ic * w].rearrange("(p c w) -> p c w", p=128, c=n_ic)

    def y_view(tt, w):
        base = tt * 128 * DC * 512
        return y_flat[base:base + 128 * DC * w].rearrange("(p c w) -> p c w", p=128, c=DC)
    xmid_fm = nc.dram_tensor("xmid_fm", [D, TC], F32).ap()
    x2_fm = nc.dram_tensor("x2_fm", [D, T], F32).ap()
    x3_fm = nc.dram_tensor("x3_fm", [D, T], F32).ap()
    convst = nc.dram_tensor("convst", [3, D, CW], BF16).ap()
    convT = nc.dram_tensor("convT", [24 * 2 * 128 * 3 * 16 * 128], BF16).ap()
    qkvT = nc.dram_tensor("qkvT", [24 * 128 * 8 * 128], BF16).ap()
    vT = nc.dram_tensor("vT", [2 * 128 * 8 * 512], BF16).ap()
    projT = nc.dram_tensor("projT", [32 * 128 * 8 * 128], BF16).ap()

    def convT_view(oc, half):
        base = (oc * 2 + half) * 128 * 3 * 16 * 128
        return convT[base:base + 128 * 3 * 16 * 128].rearrange("(p t c o) -> p t c o", p=128, t=3, c=16)

    def qkvT_view(idx):
        base = idx * 128 * 8 * 128
        return qkvT[base:base + 128 * 8 * 128].rearrange("(p w o) -> p w o", p=128, w=8)

    def vT_view(half):
        base = half * 128 * 8 * 512
        return vT[base:base + 128 * 8 * 512].rearrange("(p w o) -> p w o", p=128, w=8)

    def projT_view(dc):
        base = dc * 128 * 8 * 128
        return projT[base:base + 128 * 8 * 128].rearrange("(p w o) -> p w o", p=128, w=8)
    projst = nc.dram_tensor("projst", [WQK, D], BF16).ap()
    qkvst = nc.dram_tensor("qkvst", [4, WQK, WQK], BF16).ap()
    ln_rows = nc.dram_tensor("ln_rows", [4, TC], F32).ap()

    dbg = {}
    def dbg_dram(name):
        src = {"x_fm": (x_fm, [D, TC], F32), "y_flat": (y_flat, [5 * 128 * DC * 512], BF16),
               "gT_flat": (gT_flat, [NBLK_A * 128 * DC * 256], BF16),
               "uT_flat": (uT_flat, [NBLK_A * 128 * DC * 256], BF16),
               "stage_d": (stage_d, [D, I_], BF16), "h_flat": (h_flat, [5 * 128 * n_ic * 512], BF16),
               "xmid_fm": (xmid_fm, [D, TC], F32), "x2_fm": (x2_fm, [D, T], F32),
               "x3_fm": (x3_fm, [D, T], F32),
               "convst": (convst, [3, D, CW], BF16), "projst": (projst, [WQK, D], BF16),
               "ln_rows": (ln_rows, [4, TC], F32)}
        t, shape, dt = src[name]
        o = nc.dram_tensor("dbg_" + name, shape, dt, kind="ExternalOutput").ap()
        dbg[name] = (t, o, shape)

    for name in cfg.debug:
        dbg_dram(name)

    x_fm_v = x_fm.rearrange("(c p) t -> p c t", p=128)
    xmid_v = xmid_fm.rearrange("(c p) t -> p c t", p=128)
    x2_v = x2_fm.rearrange("(c p) t -> p c t", p=128)
    x3_v = x3_fm.rearrange("(c p) t -> p c t", p=128)

    ident_bf = nc.alloc_sbuf_tensor("ident_bf", [128, 128], BF16)
    ident_f = nc.alloc_sbuf_tensor("ident_f", [128, 128], F32)
    mask = nc.alloc_sbuf_tensor("mask", [128, 128], F32)
    ones_bf = nc.alloc_sbuf_tensor("ones_bf", [128, 2], BF16)
    ones_f = nc.alloc_sbuf_tensor("ones_f", [128, 2], F32)

    with nc.Block("init") as blk:
        @blk.gpsimd
        def _(g):
            g.memset(ones_bf[:, :], 1.0)
            g.memset(ones_f[:, :], 1.0)
            g.memset(mask[:, :], 0.0)
            g.drain()
            g.affine_select(ident_bf[:, :], ones_bf[:, 0:1].broadcast_to([128, 128]),
                            [[-1, 128]], ALU.is_equal, 0.0, base=0, channel_multiplier=1)
            g.affine_select(ident_f[:, :], ones_f[:, 0:1].broadcast_to([128, 128]),
                            [[-1, 128]], ALU.is_equal, 0.0, base=0, channel_multiplier=1)
            g.affine_select(mask[:, :], mask[:, :], [[-1, 128]], ALU.is_ge, -1e9,
                            base=0, channel_multiplier=1)

    ns0 = len(cfg.s0)
    _m_s0 = sem_mark()
    c_in = [Ch(nc, "s0_in0"), Ch(nc, "s0_in1")]
    c_sq = Ch(nc, "s0_sq")
    c_var = Ch(nc, "s0_var")
    c_srt = Ch(nc, "s0_srt")
    c_y = Ch(nc, "s0_y")
    c_tr = Ch(nc, "s0_tr")
    c_ev = Ch(nc, "s0_ev")
    c_out = [Ch(nc, "s0_out0"), Ch(nc, "s0_out1")]

    s0_stk = ExitStack()
    _out_cnt = []
    for _t in range(ns0):
        _off = cfg.s0[_t]
        _n = sum(1 for (toff, tw) in cfg.ta if max(_off, toff) < min(_off + 128, toff + tw))
        _out_cnt.append(16 * (_n + 1))
    cum_out = []
    _c = [0, 0]
    for _t in range(ns0):
        _c[_t % 2] += _out_cnt[_t]
        cum_out.append(_c[_t % 2])
    with s0_stk as stk, nc.Block("s0") as blk:
        xin = [stk.enter_context(nc.sbuf_tensor(f"s0_xin{j}", [128, D], F32)) for j in range(2)]
        sq = stk.enter_context(nc.sbuf_tensor("s0_sq_t", [128, D], F32))
        ytm = [stk.enter_context(nc.sbuf_tensor(f"s0_ytm{j}", [128, D], BF16)) for j in range(2)]
        st_ = stk.enter_context(nc.sbuf_tensor("s0_stats", [128, 16], F32))
        yout = [stk.enter_context(nc.sbuf_tensor(f"s0_yout{j}", [128, D], BF16)) for j in range(2)]
        xout = [stk.enter_context(nc.sbuf_tensor(f"s0_xout{j}", [128, D], F32)) for j in range(2)]
        ps = [stk.enter_context(nc.psum_tensor(f"s0_ps{j}", [128, 512], F32)) for j in range(2)]

        @blk.sync
        def _(sync):
            if not on("s0"): return
            for t in range(ns0):
                b = t % 2
                off = cfg.s0[t]
                lo, hi = off - 1, off + 126
                if t >= 2:
                    sync.wait_ge(c_tr.sem, 16 * (t - 1))
                if lo < 0:
                    sync.dma_start(out=xin[b][0:1, :], in_=x_hl).then_inc(c_in[b].sem, 16)
                    sync.dma_start(out=xin[b][1:128, :], in_=x_own[0:127, :]).then_inc(c_in[b].sem, 16)
                elif hi >= T:
                    sync.dma_start(out=xin[b][0:127, :], in_=x_own[lo:lo + 127, :]).then_inc(c_in[b].sem, 16)
                    sync.dma_start(out=xin[b][127:128, :], in_=x_hr).then_inc(c_in[b].sem, 16)
                else:
                    sync.dma_start(out=xin[b][0:64, :], in_=x_own[lo:lo + 64, :]).then_inc(c_in[b].sem, 16)
                    sync.dma_start(out=xin[b][64:128, :], in_=x_own[lo + 64:lo + 128, :]).then_inc(c_in[b].sem, 16)
                c_in[b].n += 32

        @blk.scalar
        def _(scalar):
            if not on("s0"): return
            for t in range(ns0):
                b = t % 2
                s = 8 * (t % 2)
                scalar.wait_ge(c_in[b].sem, 32 * (t // 2 + 1))
                if t >= 1:
                    scalar.wait_ge(c_var.sem, t)
                scalar.activation(sq[:, :], xin[b][:, :], AF.Square)
                c_sq.inc(scalar.drain())
                scalar.wait_ge(c_var.sem, t + 1)
                scalar.activation(st_[:, s + 6:s + 7], st_[:, s + 5:s + 6], AF.Sqrt)
                c_srt.inc(scalar.drain())

        @blk.vector
        def _(vector):
            if not on("s0"): return
            for t in range(ns0):
                b = t % 2
                s = 8 * (t % 2)
                vector.wait_ge(c_in[b].sem, 32 * (t // 2 + 1))
                vector.wait_ge(c_sq.sem, t + 1)
                vector.reduce_sum(st_[:, s + 0:s + 1], xin[b][:, :], axis=AX.X)
                vector.reduce_sum(st_[:, s + 1:s + 2], sq[:, :], axis=AX.X)
                vector.drain()
                vector.tensor_scalar(st_[:, s + 2:s + 3], st_[:, s + 0:s + 1], 1.0 / D, None, ALU.mult)
                vector.tensor_scalar(st_[:, s + 3:s + 4], st_[:, s + 1:s + 2], 1.0 / D, None, ALU.mult)
                vector.drain()
                vector.tensor_tensor(st_[:, s + 4:s + 5], st_[:, s + 2:s + 3], st_[:, s + 2:s + 3], ALU.mult)
                vector.drain()
                vector.tensor_tensor(st_[:, s + 5:s + 6], st_[:, s + 3:s + 4], st_[:, s + 4:s + 5], ALU.subtract)
                vector.drain()
                c_var.inc(vector.tensor_scalar(st_[:, s + 5:s + 6], st_[:, s + 5:s + 6], 1e-5, None, ALU.add))
                vector.wait_ge(c_srt.sem, t + 1)
                vector.reciprocal(st_[:, s + 7:s + 8], st_[:, s + 6:s + 7])
                vector.drain()
                if t >= 2:
                    vector.wait_ge(c_tr.sem, 16 * (t - 2) + 8)
                c_y.inc(vector.tensor_scalar(ytm[b][:, :], xin[b][:, :], st_[:, s + 2:s + 3], st_[:, s + 7:s + 8],
                                             ALU.subtract, ALU.mult))
                for r in range(16):
                    R = 16 * t + r
                    vector.wait_ge(c_tr.sem, R + 1)
                    if t >= 2 and r == 0:
                        vector.wait_ge(c_out[b].sem, cum_out[t - 2])
                    if r < 8:
                        c_ev.inc(vector.tensor_copy(yout[b][:, r * 512:(r + 1) * 512], ps[R % 2][:, :]))
                    else:
                        c_ev.inc(vector.tensor_copy(xout[b][:, (r - 8) * 512:(r - 7) * 512], ps[R % 2][:, :]))

        @blk.tensor
        def _(tensor):
            if not on("s0"): return
            for t in range(ns0):
                b = t % 2
                tensor.wait_ge(c_y.sem, t + 1)
                for r in range(16):
                    R = 16 * t + r
                    if R >= 2:
                        tensor.wait_ge(c_ev.sem, R - 1)
                    for k in range(4):
                        c = (r % 8) * 4 + k
                        if r < 8:
                            mm = tensor.matmul(ps[R % 2][:, k * 128:(k + 1) * 128],
                                               ytm[b][:, c * 128:(c + 1) * 128], ident_bf[:, :],
                                               start=True, stop=True)
                        else:
                            mm = tensor.matmul(ps[R % 2][:, k * 128:(k + 1) * 128],
                                               xin[b][:, c * 128:(c + 1) * 128], ident_f[:, :],
                                               start=True, stop=True)
                        if k == 3:
                            c_tr.inc(mm)

        @blk.gpsimd
        def _(g):
            if not on("s0"): return
            for t in range(ns0):
                b = t % 2
                off = cfg.s0[t]
                g.wait_ge(c_ev.sem, 16 * t + 8)
                if t == ns0 - 1 and t >= 1:
                    g.wait_ge(c_out[(t - 1) % 2].sem, cum_out[t - 1])
                for tt, (toff, tw) in enumerate(cfg.ta):
                    lo = max(off, toff)
                    hi = min(off + 128, toff + tw)
                    if lo >= hi:
                        continue
                    g.dma_start(out=y_view(tt, tw)[:, :, lo - toff:hi - toff],
                                in_=yout[b][:, :].rearrange("p (c t) -> p c t", t=128)[:, :, lo - off:hi - off]
                                ).then_inc(c_out[b].sem, 16)
                g.wait_ge(c_ev.sem, 16 * (t + 1))
                g.dma_start(out=x_fm_v[:, :, off:off + 128],
                            in_=xout[b][:, :].rearrange("p (c t) -> p c t", t=128)).then_inc(c_out[b].sem, 16)
                c_out[b].n = cum_out[t]
            g.wait_ge(c_out[0].sem, c_out[0].n)
            g.wait_ge(c_out[1].sem, c_out[1].n)

    sem_free(_m_s0)

    _m_s1a = sem_mark()
    deq_iter = ([("g", c) for c in range(n_ic)] + [("u", c) for c in range(n_ic)]
                + [("d", c) for c in range(DC)])
    deq_src = {"g": (gq, gsc, gzr, None, D // 2, D // 64),
               "u": (uq, usc, uzr, None, D // 2, D // 64),
               "d": (dq, dsc, dzr, stage_d, I_ // 2, I_ // 64)}
    c_qin = [Ch(nc, "s1a_qin0"), Ch(nc, "s1a_qin1")]
    c_deq = Ch(nc, "s1a_deq")
    c_stw = [Ch(nc, "s1a_stw0"), Ch(nc, "s1a_stw1")]
    c_tr1 = Ch(nc, "s1a_tr1")
    c_tev = Ch(nc, "s1a_tev")
    c_gtw = Ch(nc, "s1a_gtw")
    c_cin = [Ch(nc, "s1a_cin0"), Ch(nc, "s1a_cin1")]
    c_cast = Ch(nc, "s1a_cast")
    c_cw = [Ch(nc, "s1a_cw0"), Ch(nc, "s1a_cw1")]
    cast_iter = ([("c", w, c, hh) for w in range(3) for c in range(DC) for hh in range(2)]
                 + [("p", 0, c, hh) for c in range(WQK // 128) for hh in range(2)]
                 + [("4", m, c, hh) for m in range(4) for c in range(WQK // 128) for hh in range(2)])
    KMAX = max(D // 2, I_ // 2)
    NGMAX = max(D // 64, I_ // 64)
    with ExitStack() as stk, nc.Block("s1a") as blk:
        qin = [stk.enter_context(nc.sbuf_tensor(f"s1a_qin{j}", [128, KMAX], I32)) for j in range(2)]
        scz = [stk.enter_context(nc.sbuf_tensor(f"s1a_scz{j}", [128, 2 * NGMAX], F32)) for j in range(2)]
        ei = stk.enter_context(nc.sbuf_tensor("s1a_ei", [128, KMAX], I32))
        tmpf = stk.enter_context(nc.sbuf_tensor("s1a_tmpf", [128, KMAX], F32))
        stout = [stk.enter_context(nc.sbuf_tensor(f"s1a_stout{j}", [128, 2 * KMAX], BF16)) for j in range(2)]
        wTstg = stk.enter_context(nc.sbuf_tensor("s1a_wTstg", [128, DC, 256], BF16))
        wTstu = stk.enter_context(nc.sbuf_tensor("s1a_wTstu", [128, DC, 256], BF16))
        psT = [stk.enter_context(nc.psum_tensor(f"s1a_psT{j}", [128, 512], F32)) for j in range(2)]
        cin = [stk.enter_context(nc.sbuf_tensor(f"s1a_cin{j}", [128, max(CW, D) // 2], F32)) for j in range(2)]
        cout = [stk.enter_context(nc.sbuf_tensor(f"s1a_cout{j}", [128, max(CW, D) // 2], BF16)) for j in range(2)]

        @blk.sync
        def _(sync):
            if not on("s1a"): return
            for it, (m, c) in enumerate(deq_iter):
                sl = it % 2
                src, ssc, szr, _, K, NG = deq_src[m]
                if it >= 2:
                    sync.wait_ge(c_deq.sem, it - 1)
                sync.dma_start(out=qin[sl][:, 0:K],
                               in_=src.rearrange("(c p) k -> p c k", p=128)[:, c, :]).then_inc(c_qin[sl].sem, 16)
                sync.dma_start(out=scz[sl][:, 0:NG],
                               in_=ssc.rearrange("(c p) k -> p c k", p=128)[:, c, :]).then_inc(c_qin[sl].sem, 16)
                sync.dma_start(out=scz[sl][:, NG:2 * NG],
                               in_=szr.rearrange("(c p) k -> p c k", p=128)[:, c, :]).then_inc(c_qin[sl].sem, 16)
                c_qin[sl].n += 48

        @blk.vector
        def _(vector):
            if not on("s1a"): return
            for it, (m, c) in enumerate(deq_iter):
                sl = it % 2
                _, _, _, _, K, NG = deq_src[m]
                vector.wait_ge(c_qin[sl].sem, 48 * (it // 2 + 1))
                if it >= 2:
                    pm, _pc = deq_iter[it - 2]
                    if pm == "d":
                        j0 = (it - 2) - 2 * n_ic
                        vector.wait_ge(c_stw[sl].sem, 16 * (j0 // 2 + 1))
                    else:
                        vector.wait_ge(c_tr1.sem, 8 * (it - 1))
                sc_b = scz[sl][:, 0:NG].unsqueeze(2).broadcast_to([128, NG, 32])
                zr_b = scz[sl][:, NG:2 * NG].unsqueeze(2).broadcast_to([128, NG, 32])
                ev = stout[sl][:, 0:2 * K].rearrange("p (j two) -> p two j", two=2)
                qv = qin[sl][:, 0:K].rearrange("p (g r) -> p g r", r=32)
                tv = tmpf[:, 0:K].rearrange("p (g r) -> p g r", r=32)
                ev2 = ev[:, 0, :].rearrange("p (g r) -> p g r", r=32)
                od2 = ev[:, 1, :].rearrange("p (g r) -> p g r", r=32)
                vector.tensor_scalar(ei[:, 0:K], qin[sl][:, 0:K], 4, 15,
                                     ALU.logical_shift_right, ALU.bitwise_and)
                vector.drain()
                vector.tensor_tensor(tv, ei[:, 0:K].rearrange("p (g r) -> p g r", r=32), zr_b, ALU.subtract)
                vector.drain()
                vector.tensor_tensor(ev2, tv, sc_b, ALU.mult)
                vector.drain()
                vector.tensor_single_scalar(ei[:, 0:K], qin[sl][:, 0:K], 15, ALU.bitwise_and)
                vector.drain()
                vector.tensor_tensor(tv, ei[:, 0:K].rearrange("p (g r) -> p g r", r=32), zr_b, ALU.subtract)
                vector.drain()
                c_deq.inc(vector.tensor_tensor(od2, tv, sc_b, ALU.mult))

        @blk.scalar
        def _(scalar):
            if not on("s1a"): return
            for it, (m, c) in enumerate(deq_iter):
                sl = it % 2
                _, _, _, stg, K, NG = deq_src[m]
                if m == "d":
                    scalar.wait_ge(c_deq.sem, it + 1)
                    scalar.dma_start(out=stg.rearrange("(c p) k -> p c k", p=128)[:, c, :],
                                     in_=stout[sl][:, 0:2 * K]).then_inc(c_stw[sl].sem, 16)
                    c_stw[sl].n += 16
                else:
                    wdst = wTstg if m == "g" else wTstu
                    if c % 2 == 0 and c >= 2:
                        nw = sum(1 for (m2, c2) in deq_iter[:it] if m2 != "d" and c2 % 2 == 1)
                        scalar.wait_ge(c_gtw.sem, 16 * nw)
                    for grp in range(8):
                        gidx = it * 8 + grp
                        scalar.wait_ge(c_tr1.sem, gidx + 1)
                        scalar.activation(
                            wdst[:, grp * 4:grp * 4 + 4, (c % 2) * 128:(c % 2) * 128 + 128],
                            psT[gidx % 2][:, :], AF.Copy)
                        c_tev.inc(scalar.drain())
                    if c % 2 == 1:
                        t_ = gT_flat if m == "g" else uT_flat
                        scalar.dma_start(out=gT_view(t_, c // 2),
                                         in_=wdst[:, :, :]).then_inc(c_gtw.sem, 16)
                        c_gtw.n += 16
            for it, (m, w, c, hh) in enumerate(cast_iter):
                sl = it % 2
                scalar.wait_ge(c_cin[sl].sem, 16 * (it // 2 + 1))
                if it >= 2:
                    scalar.wait_ge(c_cw[sl].sem, 16 * (it // 2))
                n = (CW if m == "c" else (D if m == "p" else WQK)) // 2
                scalar.activation(cout[sl][:, 0:n], cin[sl][:, 0:n], AF.Copy)
                c_cast.inc(scalar.drain())
                if m == "c":
                    o = convst.rearrange("w (c p) o -> p w c o", p=128)[:, w, c, hh * n:(hh + 1) * n]
                elif m == "p":
                    o = projst.rearrange("(c p) k -> p c k", p=128)[:, c, hh * n:(hh + 1) * n]
                else:
                    o = qkvst.rearrange("m (c p) k -> p m c k", p=128)[:, w, c, hh * n:(hh + 1) * n]
                scalar.dma_start(out=o, in_=cout[sl][:, 0:n]).then_inc(c_cw[sl].sem, 16)
                c_cw[sl].n += 16

        @blk.tensor
        def _(tensor):
            if not on("s1a"): return
            for it, (m, c) in enumerate(deq_iter):
                if m == "d":
                    continue
                sl = it % 2
                tensor.wait_ge(c_deq.sem, it + 1)
                for grp in range(8):
                    gidx = it * 8 + grp
                    if gidx >= 2:
                        tensor.wait_ge(c_tev.sem, gidx - 1)
                    for k in range(4):
                        dc = grp * 4 + k
                        mm = tensor.matmul(psT[gidx % 2][:, k * 128:(k + 1) * 128],
                                           stout[sl][:, dc * 128:(dc + 1) * 128],
                                           ident_bf[:, :], start=True, stop=True)
                    c_tr1.inc(mm)

        @blk.gpsimd
        def _(g):
            if not on("s1a"): return
            for it, (m, w, c, hh) in enumerate(cast_iter):
                sl = it % 2
                if it >= 2:
                    g.wait_ge(c_cast.sem, it - 1)
                if m == "c":
                    n = CW // 2
                    src = w_conv.rearrange("w (c p) o -> p w c o", p=128)[:, w, c, hh * n:(hh + 1) * n]
                elif m == "p":
                    n = D // 2
                    src = w_p.rearrange("(c p) k -> p c k", p=128)[:, c, hh * n:(hh + 1) * n]
                else:
                    n = WQK // 2
                    src = [w_q, w_k, w_v, w_o][w].rearrange("(c p) k -> p c k", p=128)[:, c, hh * n:(hh + 1) * n]
                g.dma_start(out=cin[sl][:, 0:n], in_=src).then_inc(c_cin[sl].sem, 16)
                c_cin[sl].n += 16
            g.wait_ge(c_stw[0].sem, c_stw[0].n)
            g.wait_ge(c_stw[1].sem, c_stw[1].n)
            g.wait_ge(c_gtw.sem, c_gtw.n)
            g.wait_ge(c_cw[0].sem, c_cw[0].n)
            g.wait_ge(c_cw[1].sem, c_cw[1].n)

    sem_free(_m_s1a)

    assert n_ic % cfg.IBC == 0
    nblk = n_ic // cfg.IBC
    IBC = cfg.IBC

    def emit_ffn_A(tag, yv_fn, ta_list, extra_gp=None, extra_sc=None):
        _m = sem_mark()
        n_ta_ = len(ta_list)
        TAW_ = max(w for _, w in ta_list)
        c_wt = [Ch(nc, f"{tag}_wt{j}") for j in range(2)]
        c_yb = [Ch(nc, f"{tag}_yb{j}") for j in range(2)]
        c_mm = Ch(nc, f"{tag}_mm")
        c_sg = Ch(nc, f"{tag}_sg")
        c_hv = Ch(nc, f"{tag}_hv")
        c_hw = [Ch(nc, f"{tag}_hw{j}") for j in range(2)]

        def mm_after_tt(gt):
            return 2 * IBC * gt

        def pair_idx(b_, tt, ic):
            return (b_ * n_ta_ + tt) * IBC + ic

        with ExitStack() as stk, nc.Block(tag) as blk:
            wTg = [stk.enter_context(nc.sbuf_tensor(f"{tag}_wTg{j}", [128, DC, IBC * 128], BF16)) for j in range(2)]
            wTu = [stk.enter_context(nc.sbuf_tensor(f"{tag}_wTu{j}", [128, DC, IBC * 128], BF16)) for j in range(2)]
            ybuf = [stk.enter_context(nc.sbuf_tensor(f"{tag}_yb{j}", [128, DC, TAW_], BF16)) for j in range(2)]
            sgb = [stk.enter_context(nc.sbuf_tensor(f"{tag}_sg{j}", [128, TAW_], F32)) for j in range(2)]
            hbuf = [stk.enter_context(nc.sbuf_tensor(f"{tag}_hb{j}", [128, TAW_], BF16)) for j in range(2)]
            psgu = [stk.enter_context(nc.psum_tensor(f"{tag}_ps{j}", [128, TAW_], F32)) for j in range(4)]

            @blk.sync
            def _(sync):
                if not on(tag): return
                for b_ in range(nblk):
                    sl = b_ % 2
                    if b_ >= 2:
                        sync.wait_ge(c_mm.sem, 2 * IBC * n_ta_ * (b_ - 1))
                    sync.dma_start(out=wTg[sl][:, :, :],
                                   in_=gT_view(gT_flat, b_)).then_inc(c_wt[sl].sem, 16)
                    sync.dma_start(out=wTu[sl][:, :, :],
                                   in_=gT_view(uT_flat, b_)).then_inc(c_wt[sl].sem, 16)
                    c_wt[sl].n += 32
                    for tt in range(n_ta_):
                        gt = b_ * n_ta_ + tt
                        ysl = gt % 2
                        if gt >= 2:
                            sync.wait_ge(c_mm.sem, mm_after_tt(gt - 1))
                        off, w = ta_list[tt]
                        sync.dma_start(out=ybuf[ysl][:, :, 0:w],
                                       in_=yv_fn(tt, w)).then_inc(c_yb[ysl].sem, 16)
                        c_yb[ysl].n += 16

            @blk.scalar
            def _(scalar):
                if not on(tag): return
                if extra_sc is not None:
                    extra_sc(scalar)
                for b_ in range(nblk):
                    for tt in range(n_ta_):
                        off, w = ta_list[tt]
                        for ic in range(IBC):
                            pi = pair_idx(b_, tt, ic)
                            scalar.wait_ge(c_mm.sem, 2 * pi + 1)
                            if pi >= 2:
                                scalar.wait_ge(c_hv.sem, pi - 1)
                            scalar.activation(sgb[pi % 2][:, 0:w], psgu[2 * (pi % 2)][:, 0:w], AF.Silu)
                            c_sg.inc(scalar.drain())

            @blk.tensor
            def _(tensor):
                if not on(tag): return
                for b_ in range(nblk):
                    sl = b_ % 2
                    tensor.wait_ge(c_wt[sl].sem, 32 * (b_ // 2 + 1))
                    for tt in range(n_ta_):
                        gt = b_ * n_ta_ + tt
                        off, w = ta_list[tt]
                        tensor.wait_ge(c_yb[gt % 2].sem, 16 * (gt // 2 + 1))
                        for ic in range(IBC):
                            pi = pair_idx(b_, tt, ic)
                            if pi >= 2:
                                tensor.wait_ge(c_hv.sem, pi - 1)
                            for wt, pso in ((wTg, 0), (wTu, 1)):
                                psd = psgu[2 * (pi % 2) + pso]
                                for dc in range(DC):
                                    mm = tensor.matmul(psd[:, 0:w],
                                                       wt[sl][:, dc, ic * 128:(ic + 1) * 128],
                                                       ybuf[gt % 2][:, dc, 0:w],
                                                       start=(dc == 0), stop=(dc == DC - 1))
                                c_mm.inc(mm)

            @blk.vector
            def _(vector):
                if not on(tag): return
                for b_ in range(nblk):
                    for tt in range(n_ta_):
                        off, w = ta_list[tt]
                        for ic in range(IBC):
                            pi = pair_idx(b_, tt, ic)
                            vector.wait_ge(c_mm.sem, 2 * pi + 2)
                            vector.wait_ge(c_sg.sem, pi + 1)
                            if pi >= 2:
                                vector.wait_ge(c_hw[pi % 2].sem, 16 * (pi // 2))
                            c_hv.inc(vector.tensor_tensor(hbuf[pi % 2][:, 0:w], sgb[pi % 2][:, 0:w],
                                                          psgu[2 * (pi % 2) + 1][:, 0:w], ALU.mult))

            @blk.gpsimd
            def _(g):
                if not on(tag): return
                if extra_gp is not None:
                    extra_gp(g)
                for b_ in range(nblk):
                    for tt in range(n_ta_):
                        off, w = ta_list[tt]
                        for ic in range(IBC):
                            pi = pair_idx(b_, tt, ic)
                            g.wait_ge(c_hv.sem, pi + 1)
                            g.dma_start(out=h_view(tt, w)[:, b_ * IBC + ic, :],
                                        in_=hbuf[pi % 2][:, 0:w]).then_inc(c_hw[pi % 2].sem, 16)
                            c_hw[pi % 2].n += 16
                g.wait_ge(c_hw[0].sem, c_hw[0].n)
                g.wait_ge(c_hw[1].sem, c_hw[1].n)
        sem_free(_m)

    sup = []
    c0 = 0
    while c0 < DC:
        sup.append((c0, min(3, DC - c0)))
        c0 += min(3, DC - c0)
    NOS = len(sup)
    cumoc = [0]
    for _, co_ in sup:
        cumoc.append(cumoc[-1] + co_)
    n_hd = 1
    NSLOT = 16

    def emit_ffn_B(tag, x_view, xout_view, ta_list, stats_rows):
        _m = sem_mark()
        n_ta_ = len(ta_list)
        TAW_ = max(w for _, w in ta_list)
        c_ht = [Ch(nc, f"{tag}_ht{j}") for j in range(2)]
        c_wds = [Ch(nc, f"{tag}_wd{j}") for j in range(NSLOT)]
        c_mmi = Ch(nc, f"{tag}_mmi")
        c_stt = Ch(nc, f"{tag}_stt")
        c_cast = Ch(nc, f"{tag}_cast")
        c_mmst = Ch(nc, f"{tag}_mmst")
        c_fin = Ch(nc, f"{tag}_fin")
        c_xs = [Ch(nc, f"{tag}_xs{j}") for j in range(2)]
        c_xw = [Ch(nc, f"{tag}_xw{j}") for j in range(2)]
        c_lnw = Ch(nc, f"{tag}_lnw")

        def mmi_after(tt, os_, ic):
            return (tt * NOS + os_) * n_ic + ic + 1

        def k_oc(tt, os_, j):
            return tt * DC + cumoc[os_] + j

        def sttc(gi):
            if gi < 0:
                return 0
            return (gi // NOS) * DC + cumoc[gi % NOS + 1]

        with ExitStack() as stk, nc.Block(tag) as blk:
            htile = [stk.enter_context(nc.sbuf_tensor(f"{tag}_ht{j}", [128, n_ic, TAW_], BF16)) for j in range(2)]
            wds = [stk.enter_context(nc.sbuf_tensor(f"{tag}_wd{j}", [128, 384], BF16)) for j in range(NSLOT)]
            xs = [stk.enter_context(nc.sbuf_tensor(f"{tag}_xs{j}", [128, 3, TAW_], F32)) for j in range(2)]
            xo = [stk.enter_context(nc.sbuf_tensor(f"{tag}_xo{j}", [128, 3, TAW_], F32)) for j in range(2)]
            if stats_rows is not None:
                xmbf = [stk.enter_context(nc.sbuf_tensor(f"{tag}_xmbf{j}", [128, TAW_], BF16)) for j in range(4)]
                sqbf = [stk.enter_context(nc.sbuf_tensor(f"{tag}_sqbf{j}", [128, TAW_], BF16)) for j in range(4)]
                srow = stk.enter_context(nc.sbuf_tensor(f"{tag}_srow", [1, 6 * TAW_], F32))
                psS = [stk.enter_context(nc.psum_tensor(f"{tag}_psS{j}", [1, TAW_], F32)) for j in range(2)]
            ps_o = [[stk.enter_context(nc.psum_tensor(f"{tag}_ps{j}_{k}", [128, TAW_], F32)) for k in range(3)]
                    for j in range(2)]

            @blk.sync
            def _(sync):
                if not on(tag): return
                for tt in range(n_ta_):
                    for os_, (oc0, co) in enumerate(sup):
                        for ic in range(n_ic):
                            si = (tt * NOS + os_) * n_ic + ic
                            sl = si % NSLOT
                            if si >= NSLOT:
                                sync.wait_ge(c_mmi.sem, si - (NSLOT - 2))
                            sync.dma_start(out=wds[sl][:, 0:co * 128],
                                           in_=stage_d[oc0 * 128:(oc0 + co) * 128,
                                                       ic * 128:(ic + 1) * 128],
                                           transpose=True).then_inc(c_wds[sl].sem, 16)
                            c_wds[sl].n += 16

            @blk.scalar
            def _(scalar):
                if not on(tag): return
                for tt in range(n_ta_):
                    off, w = ta_list[tt]
                    if stats_rows is None:
                        continue
                    for os_, (oc0, co) in enumerate(sup):
                        for j in range(co):
                            k = k_oc(tt, os_, j)
                            gi_ = tt * NOS + os_
                            scalar.wait_ge(c_stt.sem, k + 1)
                            if k >= 4:
                                scalar.wait_ge(c_mmst.sem, k - 3)
                            scalar.activation(xmbf[k % 4][:, 0:w], xo[gi_ % 2][:, j, 0:w], AF.Copy)
                            scalar.drain()
                            scalar.activation(sqbf[k % 4][:, 0:w], xo[gi_ % 2][:, j, 0:w], AF.Square)
                            c_cast.inc(scalar.drain())
                    scalar.wait_ge(c_fin.sem, 4 * tt + 2)
                    scalar.activation(srow[:, 4 * TAW_:4 * TAW_ + w], srow[:, 2 * TAW_:2 * TAW_ + w], AF.Sqrt)
                    c_fin.inc(scalar.drain())

            @blk.tensor
            def _(tensor):
                if not on(tag): return
                for tt in range(n_ta_):
                    off, w = ta_list[tt]
                    tensor.wait_ge(c_ht[tt % 2].sem, 16 * n_hd * (tt // 2 + 1))

                    def stats_mms(tt_, po):
                        for j in range(sup[po][1]):
                            k = k_oc(tt_, po, j)
                            tensor.wait_ge(c_cast.sem, k + 1)
                            if k % DC == 0 and tt_ >= 1:
                                tensor.wait_ge(c_fin.sem, 4 * (tt_ - 1) + 1)
                            tensor.matmul(psS[0][0:1, 0:w], ones_bf[:, 0:1], xmbf[k % 4][:, 0:w],
                                          start=(k % DC == 0), stop=(k % DC == DC - 1))
                            mm = tensor.matmul(psS[1][0:1, 0:w], ones_bf[:, 0:1], sqbf[k % 4][:, 0:w],
                                               start=(k % DC == 0), stop=(k % DC == DC - 1))
                            c_mmst.inc(mm)

                    for os_, (oc0, co) in enumerate(sup):
                        gi = tt * NOS + os_
                        pso = ps_o[gi % 2]
                        if gi >= 2:
                            tensor.wait_ge(c_stt.sem, sttc(gi - 2))
                        for ic in range(n_ic):
                            si = gi * n_ic + ic
                            sl = si % NSLOT
                            tensor.wait_ge(c_wds[sl].sem, 16 * (si // NSLOT + 1))
                            for j in range(co):
                                mm = tensor.matmul(pso[j][:, 0:w],
                                                   wds[sl][:, j * 128:(j + 1) * 128],
                                                   htile[tt % 2][:, ic, 0:w],
                                                   start=(ic == 0), stop=(ic == n_ic - 1))
                            c_mmi.inc(mm)
                        if stats_rows is not None and os_ >= 1:
                            stats_mms(tt, os_ - 1)
                    if stats_rows is not None:
                        stats_mms(tt, NOS - 1)

            @blk.vector
            def _(vector):
                if not on(tag): return
                for tt in range(n_ta_):
                    off, w = ta_list[tt]
                    for os_, (oc0, co) in enumerate(sup):
                        gi = tt * NOS + os_
                        vector.wait_ge(c_mmi.sem, mmi_after(tt, os_, n_ic - 1))
                        vector.wait_ge(c_xs[gi % 2].sem, 16 * (gi // 2 + 1))
                        if gi >= 2:
                            vector.wait_ge(c_xw[gi % 2].sem, 16 * (gi // 2))
                            if stats_rows is not None:
                                vector.wait_ge(c_cast.sem, sttc(gi - 2))
                        for j in range(co):
                            c_stt.inc(vector.scalar_tensor_tensor(
                                xo[gi % 2][:, j, 0:w], ps_o[gi % 2][j][:, 0:w], 0.5,
                                xs[gi % 2][:, j, 0:w], ALU.mult, ALU.add))
                    if stats_rows is None:
                        continue
                    if tt >= 1:
                        vector.wait_ge(c_lnw.sem, 32 * tt)
                    vector.wait_ge(c_mmst.sem, DC * (tt + 1))
                    vector.tensor_scalar(srow[:, 0:w], psS[0][0:1, 0:w], 1.0 / D, None, ALU.mult)
                    vector.tensor_scalar(srow[:, TAW_:TAW_ + w], psS[1][0:1, 0:w], 1.0 / D, None, ALU.mult)
                    c_fin.inc(vector.drain())
                    vector.tensor_tensor(srow[:, 2 * TAW_:2 * TAW_ + w], srow[:, 0:w], srow[:, 0:w], ALU.mult)
                    vector.drain()
                    vector.tensor_tensor(srow[:, 2 * TAW_:2 * TAW_ + w], srow[:, TAW_:TAW_ + w],
                                         srow[:, 2 * TAW_:2 * TAW_ + w], ALU.subtract)
                    vector.drain()
                    vector.tensor_scalar(srow[:, 2 * TAW_:2 * TAW_ + w], srow[:, 2 * TAW_:2 * TAW_ + w],
                                         1e-5, None, ALU.add)
                    c_fin.inc(vector.drain())
                    vector.wait_ge(c_fin.sem, 4 * tt + 3)
                    vector.reciprocal(srow[:, 5 * TAW_:5 * TAW_ + w], srow[:, 4 * TAW_:4 * TAW_ + w])
                    c_fin.inc(vector.drain())

            @blk.gpsimd
            def _(g):
                if not on(tag): return
                for tt in range(n_ta_):
                    off, w = ta_list[tt]
                    if tt == 0:
                        for t2 in (0, 1):
                            if t2 >= n_ta_:
                                continue
                            o2, w2 = ta_list[t2]
                            g.dma_start(out=htile[t2 % 2][:, :, 0:w2],
                                        in_=h_view(t2, w2)).then_inc(c_ht[t2 % 2].sem, 16)
                            c_ht[t2 % 2].n += 16
                    for os_, (oc0, co) in enumerate(sup):
                        gi = tt * NOS + os_
                        if gi >= 2:
                            g.wait_ge(c_stt.sem, sttc(gi - 2))
                        g.dma_start(out=xs[gi % 2][:, 0:co, 0:w],
                                    in_=x_view[:, oc0:oc0 + co, off:off + w]).then_inc(c_xs[gi % 2].sem, 16)
                        c_xs[gi % 2].n += 16
                        g.wait_ge(c_stt.sem, sttc(gi))
                        g.dma_start(out=xout_view[:, oc0:oc0 + co, off:off + w],
                                    in_=xo[gi % 2][:, 0:co, 0:w]).then_inc(c_xw[gi % 2].sem, 16)
                        c_xw[gi % 2].n += 16
                    if stats_rows is not None:
                        mrow, rrow = stats_rows
                        g.wait_ge(c_fin.sem, 4 * (tt + 1))
                        g.dma_start(out=ln_rows[mrow:mrow + 1, off:off + w],
                                    in_=srow[:, 0:w]).then_inc(c_lnw.sem, 16)
                        g.dma_start(out=ln_rows[rrow:rrow + 1, off:off + w],
                                    in_=srow[:, 5 * TAW_:5 * TAW_ + w]).then_inc(c_lnw.sem, 16)
                        c_lnw.n += 32
                    if tt + 2 < n_ta_:
                        t2 = tt + 2
                        o2, w2 = ta_list[t2]
                        g.wait_ge(c_mmi.sem, mmi_after(tt, NOS - 1, n_ic - 1))
                        g.dma_start(out=htile[t2 % 2][:, :, 0:w2],
                                    in_=h_view(t2, w2)).then_inc(c_ht[t2 % 2].sem, 16)
                        c_ht[t2 % 2].n += 16
                g.wait_ge(c_lnw.sem, c_lnw.n)
                g.wait_ge(c_xw[0].sem, c_xw[0].n)
                g.wait_ge(c_xw[1].sem, c_xw[1].n)
        sem_free(_m)

    c_rr = Ch(nc, "s1r_rr")
    c_rw = Ch(nc, "s1r_rw")
    rp_iter = ([("c", oc, 0) for oc in range(24)]
               + [("q", mi, hc) for mi, m in enumerate((0, 1, 3)) for hc in range(8)]
               + [("v", h_, 0) for h_ in range(2)]
               + [("p", dc, 0) for dc in range(32)])
    rp_reads = [6 if m == "c" else 1 for (m, a, b2) in rp_iter]
    rp_cum = []
    _cc = 0
    for r_ in rp_reads:
        _cc += 16 * r_
        rp_cum.append(_cc)
    qkvst_v = qkvst.rearrange("m (c p) k -> p m c k", p=128)
    projst_v = projst.rearrange("(c p) k -> p c k", p=128)
    convst_v = convst.rearrange("w (c p) o -> p w c o", p=128)

    rp_writes = [32 if m == "c" else 16 for (m, a, b2) in rp_iter]
    rw_cum = []
    _cw = 0
    for r_ in rp_writes:
        _cw += r_
        rw_cum.append(_cw)

    def _rp_gp(g):
        for it_, (m, a, b2) in enumerate(rp_iter):
            if it_ >= 1:
                g.wait_ge(c_rw.sem, rw_cum[it_ - 1])
            if m == "c":
                rv = rp_buf[:, 0:12288].rearrange("p (h t c o) -> p h t c o", h=2, t=3, c=16)
                for tap in range(3):
                    for h_ in range(2):
                        g.dma_start(out=rv[:, h_, tap, :, :],
                                    in_=convst_v[:, tap, 16 * h_:16 * h_ + 16, a * 128:(a + 1) * 128]
                                    ).then_inc(c_rr.sem, 16)
            elif m == "q":
                g.dma_start(out=rp_buf[:, 0:1024],
                            in_=qkvst_v[:, (0, 1, 3)[a], :, b2 * 128:(b2 + 1) * 128]
                            ).then_inc(c_rr.sem, 16)
            elif m == "v":
                g.dma_start(out=rp_buf[:, 0:4096],
                            in_=qkvst_v[:, 2, :, a * 512:(a + 1) * 512]).then_inc(c_rr.sem, 16)
            else:
                g.dma_start(out=rp_buf[:, 0:1024],
                            in_=projst_v[:, :, a * 128:(a + 1) * 128]).then_inc(c_rr.sem, 16)
        c_rr.n = rp_cum[-1]

    def _rp_sc(scalar):
        for it_, (m, a, b2) in enumerate(rp_iter):
            scalar.wait_ge(c_rr.sem, rp_cum[it_])
            if m == "c":
                for half in range(2):
                    scalar.dma_start(out=convT[((a * 2 + half) * 128 * 6144):((a * 2 + half + 1) * 128 * 6144)]
                                     .rearrange("(p x) -> p x", p=128),
                                     in_=rp_buf[:, half * 6144:(half + 1) * 6144]
                                     ).then_inc(c_rw.sem, 16)
            elif m == "q":
                scalar.dma_start(out=qkvT_view(a * 8 + b2).rearrange("p w o -> p (w o)"),
                                 in_=rp_buf[:, 0:1024]).then_inc(c_rw.sem, 16)
            elif m == "v":
                scalar.dma_start(out=vT_view(a).rearrange("p w o -> p (w o)"),
                                 in_=rp_buf[:, 0:4096]).then_inc(c_rw.sem, 16)
            else:
                scalar.dma_start(out=projT_view(a).rearrange("p w o -> p (w o)"),
                                 in_=rp_buf[:, 0:1024]).then_inc(c_rw.sem, 16)
            c_rw.n = rw_cum[it_]
        scalar.wait_ge(c_rw.sem, c_rw.n)

    with nc.sbuf_tensor("s1r_buf", [128, 12288], BF16) as rp_buf:
        emit_ffn_A("s1b", y_view, cfg.ta, extra_gp=_rp_gp, extra_sc=_rp_sc)
    emit_ffn_B("s2", x_fm_v, xmid_v, cfg.ta, (0, 1))

    sup3 = []
    c0 = 0
    while c0 < T:
        sup3.append((c0, min(512, T - c0)))
        c0 += min(512, T - c0)
    NSUP = len(sup3)
    tb = []
    n_tb = math.ceil(T / 410)
    wb = math.ceil(T / n_tb)
    c0 = 0
    while c0 < T:
        ww = min(wb, T - c0)
        tb.append((c0, ww))
        c0 += ww

    _m_s3 = sem_mark()
    c3_pre = Ch(nc, "s3_pre")
    c3_lnb = [Ch(nc, f"s3_lnb{j}") for j in range(2)]
    c3_xm = [Ch(nc, f"s3_xm{j}") for j in range(2)]
    c3_y2 = Ch(nc, "s3_y2")
    c3_cw = [Ch(nc, f"s3_cw{j}") for j in range(2)]
    c3_cmh = Ch(nc, "s3_cmh")
    c3_qw = [Ch(nc, f"s3_qw{j}") for j in range(6)]
    c3_vw = [Ch(nc, f"s3_vw{j}") for j in range(2)]
    c3_ow = [Ch(nc, f"s3_ow{j}") for j in range(3)]
    c3_cev = Ch(nc, "s3_cev")
    c3_qk = Ch(nc, "s3_qk")
    c3_qkev = Ch(nc, "s3_qkev")
    c3_v = Ch(nc, "s3_v")
    c3_vev = Ch(nc, "s3_vev")
    c3_sc = Ch(nc, "s3_sc")
    c3_sm1 = Ch(nc, "s3_sm1")
    c3_exp = Ch(nc, "s3_exp")
    c3_p = Ch(nc, "s3_p")
    c3_pt = Ch(nc, "s3_pt")
    c3_ptev = Ch(nc, "s3_ptev")
    c3_o = Ch(nc, "s3_o")
    c3_oev = Ch(nc, "s3_oev")
    c3_wo = Ch(nc, "s3_wo")
    c3_woev = Ch(nc, "s3_woev")
    c3_pj = [Ch(nc, f"s3_pj{j}") for j in range(4)]
    c3_pjmm = Ch(nc, "s3_pjmm")
    c3_pjev = Ch(nc, "s3_pjev")
    c3_x2 = Ch(nc, "s3_x2")
    c3_xr = [Ch(nc, f"s3_xr{j}") for j in range(2)]
    c3_x2w = [Ch(nc, f"s3_x2w{j}") for j in range(2)]
    c3_c3 = Ch(nc, "s3_cast")
    c3_mmst = Ch(nc, "s3_mmst")
    c3_fin = Ch(nc, "s3_fin")
    c3_lnw = Ch(nc, "s3_lnw")

    with ExitStack() as stk, nc.Block("s3") as blk:
        qksl = [stk.enter_context(nc.sbuf_tensor(f"s3_qksl{j}", [128, 8, 128], BF16)) for j in range(6)]
        vsl = [stk.enter_context(nc.sbuf_tensor(f"s3_vsl{j}", [128, 8, 512], BF16)) for j in range(2)]
        osl = [stk.enter_context(nc.sbuf_tensor(f"s3_osl{j}", [128, 8, 128], BF16)) for j in range(3)]
        bqs = stk.enter_context(nc.sbuf_tensor("s3_bqs", [128, 8], F32))
        bks = stk.enter_context(nc.sbuf_tensor("s3_bks", [128, 8], F32))
        bos = stk.enter_context(nc.sbuf_tensor("s3_bos", [128, 8], F32))
        pbs = stk.enter_context(nc.sbuf_tensor("s3_pbs", [128, 32], F32))
        bcs = stk.enter_context(nc.sbuf_tensor("s3_bcs", [128, 24], F32))
        bvb = stk.enter_context(nc.sbuf_tensor("s3_bvb", [128, WQK], F32))
        m2b = stk.enter_context(nc.sbuf_tensor("s3_m2b", [128, 514], F32))
        r2b = stk.enter_context(nc.sbuf_tensor("s3_r2b", [128, 514], F32))
        xmc = [stk.enter_context(nc.sbuf_tensor(f"s3_xmc{j}", [128, 514], F32)) for j in range(2)]
        tmpc = [stk.enter_context(nc.sbuf_tensor(f"s3_tmpc{j}", [128, 514], F32)) for j in range(2)]
        y2su = stk.enter_context(nc.sbuf_tensor("s3_y2su", [128, DC, 514], BF16))
        cwt = [stk.enter_context(nc.sbuf_tensor(f"s3_cwt{j}", [128, 3, 16, 128], BF16)) for j in range(2)]
        csu = stk.enter_context(nc.sbuf_tensor("s3_csu", [128, 24, 512], BF16))
        qhsb = stk.enter_context(nc.sbuf_tensor("s3_qhsb", [128, 8, 128], BF16))
        khsb = stk.enter_context(nc.sbuf_tensor("s3_khsb", [128, 8, 128], BF16))
        vtsb = stk.enter_context(nc.sbuf_tensor("s3_vtsb", [128, WQK], BF16))
        ssb = [stk.enter_context(nc.sbuf_tensor(f"s3_ssb{j}", [128, 128], F32)) for j in range(2)]
        nmx = stk.enter_context(nc.sbuf_tensor("s3_nmx", [128, 8], F32))
        smr = stk.enter_context(nc.sbuf_tensor("s3_smr", [128, 8], F32))
        esb = [stk.enter_context(nc.sbuf_tensor(f"s3_esb{j}", [128, 128], F32)) for j in range(2)]
        pbf = [stk.enter_context(nc.sbuf_tensor(f"s3_pbf{j}", [128, 128], BF16)) for j in range(2)]
        ptb = [stk.enter_context(nc.sbuf_tensor(f"s3_ptb{j}", [128, 128], BF16)) for j in range(2)]
        osb = stk.enter_context(nc.sbuf_tensor("s3_osb", [128, 8, 128], BF16))
        apre = stk.enter_context(nc.sbuf_tensor("s3_apre", [128, 8, 512], BF16))
        pjs = [stk.enter_context(nc.sbuf_tensor(f"s3_pjs{j}", [128, 8, 128], BF16)) for j in range(4)]
        ptmp = [stk.enter_context(nc.sbuf_tensor(f"s3_ptmp{j}", [128, 512], F32)) for j in range(2)]
        xrs = [stk.enter_context(nc.sbuf_tensor(f"s3_xrs{j}", [128, 512], F32)) for j in range(2)]
        x2c = [stk.enter_context(nc.sbuf_tensor(f"s3_x2c{j}", [128, 512], F32)) for j in range(2)]
        x2bf = [stk.enter_context(nc.sbuf_tensor(f"s3_x2bf{j}", [128, 512], BF16)) for j in range(4)]
        sq3 = [stk.enter_context(nc.sbuf_tensor(f"s3_sq3{j}", [128, 512], BF16)) for j in range(4)]
        srow3 = stk.enter_context(nc.sbuf_tensor("s3_srow3", [1, 6 * 512], F32))
        pA = stk.enter_context(nc.psum_tensor("s3_pA", [128, 512], F32))
        pB = stk.enter_context(nc.psum_tensor("s3_pB", [128, 512], F32))
        p128 = [stk.enter_context(nc.psum_tensor(f"s3_p{j}", [128, 128], F32)) for j in range(4)]
        psS3 = [stk.enter_context(nc.psum_tensor(f"s3_psS{j}", [1, 512], F32)) for j in range(2)]

        nb_of = [sw // 128 for _, sw in sup3]
        cum_nb = [0]
        for x in nb_of:
            cum_nb.append(cum_nb[-1] + x)
        NBLK = cum_nb[-1]

        def blk_gidx(s, b):
            return cum_nb[s] + b

        @blk.sync
        def _(sync):
            if not on("s3"): return
            sync.dma_start(out=bqs[:, 0:8], in_=b_q.rearrange("(c p) -> p c", p=128), allow_slow_non_contiguous=True).then_inc(c3_pre.sem, 16)
            sync.dma_start(out=bks[:, 0:8], in_=b_k.rearrange("(c p) -> p c", p=128), allow_slow_non_contiguous=True).then_inc(c3_pre.sem, 16)
            sync.dma_start(out=bos[:, 0:8], in_=b_o.rearrange("(c p) -> p c", p=128), allow_slow_non_contiguous=True).then_inc(c3_pre.sem, 16)
            sync.dma_start(out=pbs[:, 0:32], in_=b_p.rearrange("(c p) -> p c", p=128), allow_slow_non_contiguous=True).then_inc(c3_pre.sem, 16)
            sync.dma_start(out=bcs[:, 0:24], in_=b_conv.rearrange("(c p) -> p c", p=128), allow_slow_non_contiguous=True).then_inc(c3_pre.sem, 16)
            sync.dma_start(out=bvb[:, :], in_=b_v.unsqueeze(0).broadcast_to([128, WQK])).then_inc(c3_pre.sem, 16)
            c3_pre.n += 96
            for s, (off_t, sw) in enumerate(sup3):
                for oc in range(24):
                    for half in range(2):
                        ci2 = (s * 24 + oc) * 2 + half
                        if ci2 >= 2:
                            sync.wait_ge(c3_cmh.sem, ci2 - 1)
                        sync.dma_start(out=cwt[ci2 % 2][:, :, :, :],
                                       in_=convT_view(oc, half)).then_inc(c3_cw[ci2 % 2].sem, 16)
                        c3_cw[ci2 % 2].n += 16
                for b in range(nb_of[s]):
                    gb = blk_gidx(s, b)
                    for m in range(2):
                        for hc in range(8):
                            qi = (gb * 2 + m) * 8 + hc
                            sl = qi % 6
                            if qi >= 6:
                                sync.wait_ge(c3_qk.sem, qi - 4)
                            sync.dma_start(out=qksl[sl][:, :, :],
                                           in_=qkvT_view(m * 8 + hc)).then_inc(c3_qw[sl].sem, 16)
                            c3_qw[sl].n += 16
                    for half in range(2):
                        vi = gb * 2 + half
                        if vi >= 2:
                            sync.wait_ge(c3_v.sem, vi - 1)
                        sync.dma_start(out=vsl[vi % 2][:, :, :],
                                       in_=vT_view(half)).then_inc(c3_vw[vi % 2].sem, 16)
                        c3_vw[vi % 2].n += 16
                    for oc8 in range(8):
                        wi = gb * 8 + oc8
                        sl = wi % 3
                        if wi >= 3:
                            sync.wait_ge(c3_wo.sem, wi - 2)
                        sync.dma_start(out=osl[sl][:, :, :],
                                       in_=qkvT_view(16 + oc8)).then_inc(c3_ow[sl].sem, 16)
                        c3_ow[sl].n += 16
                for dc in range(DC):
                    pi = s * DC + dc
                    sl = pi % 4
                    if pi >= 4:
                        sync.wait_ge(c3_pjmm.sem, pi - 3)
                    sync.dma_start(out=pjs[sl][:, :, :],
                                   in_=projT_view(dc)).then_inc(c3_pj[sl].sem, 16)
                    c3_pj[sl].n += 16

        @blk.gpsimd
        def _(g):
            if not on("s3"): return
            for s, (off_t, sw) in enumerate(sup3):
                if s >= 1:
                    g.wait_ge(c3_y2.sem, DC * s)
                g.dma_start(out=m2b[:, 0:sw + 2],
                            in_=ln_rows[0:1, off_t:off_t + sw + 2].broadcast_to([128, sw + 2])
                            ).then_inc(c3_lnb[s % 2].sem, 16)
                g.dma_start(out=r2b[:, 0:sw + 2],
                            in_=ln_rows[1:2, off_t:off_t + sw + 2].broadcast_to([128, sw + 2])
                            ).then_inc(c3_lnb[s % 2].sem, 16)
                c3_lnb[s % 2].n += 32
                for dc in range(DC):
                    ki = s * DC + dc
                    if ki >= 2:
                        g.wait_ge(c3_y2.sem, ki - 1)
                    g.dma_start(out=xmc[ki % 2][:, 0:sw + 2],
                                in_=xmid_v[:, dc, off_t:off_t + sw + 2]).then_inc(c3_xm[ki % 2].sem, 16)
                    c3_xm[ki % 2].n += 16
                for dc in range(DC):
                    pi = s * DC + dc
                    if pi >= 2:
                        g.wait_ge(c3_x2.sem, pi - 1)
                    g.dma_start(out=xrs[pi % 2][:, 0:sw],
                                in_=xmid_v[:, dc, off_t + 1:off_t + 1 + sw]).then_inc(c3_xr[pi % 2].sem, 16)
                    c3_xr[pi % 2].n += 16
                    g.wait_ge(c3_mmst.sem, pi + 1)
                    g.dma_start(out=x2_v[:, dc, off_t:off_t + sw],
                                in_=x2c[pi % 2][:, 0:sw]).then_inc(c3_x2w[pi % 2].sem, 16)
                    c3_x2w[pi % 2].n += 16
                g.wait_ge(c3_fin.sem, 4 * (s + 1))
                g.dma_start(out=ln_rows[2:3, off_t:off_t + sw], in_=srow3[:, 0:sw]).then_inc(c3_lnw.sem, 16)
                g.dma_start(out=ln_rows[3:4, off_t:off_t + sw],
                            in_=srow3[:, 5 * 512:5 * 512 + sw]).then_inc(c3_lnw.sem, 16)
                c3_lnw.n += 32
            g.wait_ge(c3_lnw.sem, c3_lnw.n)
            g.wait_ge(c3_x2w[0].sem, c3_x2w[0].n)
            g.wait_ge(c3_x2w[1].sem, c3_x2w[1].n)
            g.wait_ge(c3_cw[0].sem, c3_cw[0].n)
            g.wait_ge(c3_cw[1].sem, c3_cw[1].n)

        @blk.vector
        def _(vector):
            if not on("s3"): return
            vector.wait_ge(c3_pre.sem, c3_pre.n)
            vector.tensor_scalar(bqs[:, 0:8], bqs[:, 0:8], 0.125, None, ALU.mult)
            vector.drain()
            for s, (off_t, sw) in enumerate(sup3):
                vector.wait_ge(c3_lnb[s % 2].sem, 32 * (s // 2 + 1))
                if s >= 1:
                    vector.wait_ge(c3_cmh.sem, 48 * s)
                for dc in range(DC):
                    ki = s * DC + dc
                    vector.wait_ge(c3_xm[ki % 2].sem, 16 * (ki // 2 + 1))
                    vector.tensor_tensor(tmpc[ki % 2][:, 0:sw + 2], xmc[ki % 2][:, 0:sw + 2],
                                         m2b[:, 0:sw + 2], ALU.subtract)
                    vector.drain()
                    c3_y2.inc(vector.tensor_tensor(y2su[:, dc, 0:sw + 2], tmpc[ki % 2][:, 0:sw + 2],
                                                   r2b[:, 0:sw + 2], ALU.mult))
                for b in range(nb_of[s]):
                    gb = blk_gidx(s, b)
                    for half in range(2):
                        vector.wait_ge(c3_v.sem, 2 * gb + half + 1)
                        c3_vev.inc(vector.tensor_tensor(
                            vtsb[:, half * 512:(half + 1) * 512],
                            (pA if half == 0 else pB)[:, :],
                            bvb[:, half * 512:(half + 1) * 512], ALU.add))
                    for h in range(16):
                        hi_ = gb * 16 + h
                        vector.wait_ge(c3_sc.sem, hi_ + 1)
                        vector.tensor_tensor(ssb[h % 2][:, :], p128[h % 2][:, :], mask[:, :], ALU.add)
                        vector.drain()
                        vector.reduce_max(nmx[:, h % 8:h % 8 + 1], ssb[h % 2][:, :], axis=AX.X, negate=True)
                        c3_sm1.inc(vector.drain())
                        vector.wait_ge(c3_exp.sem, hi_ + 1)
                        vector.reduce_sum(smr[:, h % 8:h % 8 + 1], esb[h % 2][:, :], axis=AX.X)
                        vector.drain()
                        vector.reciprocal(smr[:, h % 8:h % 8 + 1], smr[:, h % 8:h % 8 + 1])
                        vector.drain()
                        if hi_ >= 2:
                            vector.wait_ge(c3_pt.sem, hi_ - 1)
                        c3_p.inc(vector.tensor_scalar(pbf[h % 2][:, :], esb[h % 2][:, :],
                                                      smr[:, h % 8:h % 8 + 1], None, ALU.mult))
                        vector.wait_ge(c3_pt.sem, hi_ + 1)
                        if hi_ >= 2:
                            vector.wait_ge(c3_o.sem, hi_ - 1)
                        c3_ptev.inc(vector.tensor_copy(ptb[h % 2][:, :], p128[2 + h % 2][:, :]))
                for dc in range(DC):
                    pi = s * DC + dc
                    vector.wait_ge(c3_pjev.sem, pi + 1)
                    vector.wait_ge(c3_xr[pi % 2].sem, 16 * (pi // 2 + 1))
                    if pi >= 2:
                        vector.wait_ge(c3_x2w[pi % 2].sem, 16 * (pi // 2))
                        vector.wait_ge(c3_c3.sem, pi - 1)
                    c3_x2.inc(vector.tensor_tensor(x2c[pi % 2][:, 0:sw], ptmp[pi % 2][:, 0:sw],
                                                   xrs[pi % 2][:, 0:sw], ALU.add))
                if s >= 1:
                    vector.wait_ge(c3_lnw.sem, 32 * s)
                vector.wait_ge(c3_mmst.sem, DC * (s + 1))
                vector.tensor_scalar(srow3[:, 0:sw], psS3[0][0:1, 0:sw], 1.0 / D, None, ALU.mult)
                vector.tensor_scalar(srow3[:, 512:512 + sw], psS3[1][0:1, 0:sw], 1.0 / D, None, ALU.mult)
                c3_fin.inc(vector.drain())
                vector.tensor_tensor(srow3[:, 2 * 512:2 * 512 + sw], srow3[:, 0:sw], srow3[:, 0:sw], ALU.mult)
                vector.drain()
                vector.tensor_tensor(srow3[:, 2 * 512:2 * 512 + sw], srow3[:, 512:512 + sw],
                                     srow3[:, 2 * 512:2 * 512 + sw], ALU.subtract)
                vector.drain()
                vector.tensor_scalar(srow3[:, 2 * 512:2 * 512 + sw], srow3[:, 2 * 512:2 * 512 + sw],
                                     1e-5, None, ALU.add)
                c3_fin.inc(vector.drain())
                vector.wait_ge(c3_fin.sem, 4 * s + 3)
                vector.reciprocal(srow3[:, 5 * 512:5 * 512 + sw], srow3[:, 4 * 512:4 * 512 + sw])
                c3_fin.inc(vector.drain())

        @blk.scalar
        def _(scalar):
            if not on("s3"): return
            for s, (off_t, sw) in enumerate(sup3):
                for oc in range(24):
                    ci = s * 24 + oc
                    scalar.wait_ge(c3_cmh.sem, 2 * (ci + 1))
                    scalar.activation(csu[:, oc, 0:sw], (pA if oc % 2 == 0 else pB)[:, 0:sw],
                                      AF.Identity, bias=bcs[:, oc:oc + 1])
                    c3_cev.inc(scalar.drain())
                for b in range(nb_of[s]):
                    gb = blk_gidx(s, b)
                    for m in range(2):
                        for hc in range(8):
                            qi = (gb * 2 + m) * 8 + hc
                            scalar.wait_ge(c3_qk.sem, qi + 1)
                            if m == 0:
                                scalar.activation(qhsb[:, hc, :], p128[hc % 2][:, :], AF.Identity,
                                                  bias=bqs[:, hc:hc + 1], scale=0.125)
                            else:
                                scalar.activation(khsb[:, hc, :], p128[2 + hc % 2][:, :], AF.Identity,
                                                  bias=bks[:, hc:hc + 1])
                            c3_qkev.inc(scalar.drain())
                    for h in range(16):
                        hi_ = gb * 16 + h
                        scalar.wait_ge(c3_sm1.sem, hi_ + 1)
                        scalar.activation(esb[h % 2][:, :], ssb[h % 2][:, :], AF.Exp,
                                          bias=nmx[:, h % 8:h % 8 + 1])
                        c3_exp.inc(scalar.drain())
                        if h % 2 == 1:
                            scalar.wait_ge(c3_o.sem, hi_ + 1)
                            scalar.activation(osb[:, h // 2, :], (pA if (h // 2) % 2 == 0 else pB)[:, 0:128],
                                              AF.Copy)
                            c3_oev.inc(scalar.drain())
                    for oc8 in range(8):
                        wi = gb * 8 + oc8
                        scalar.wait_ge(c3_wo.sem, wi + 1)
                        scalar.activation(apre[:, oc8, b * 128:(b + 1) * 128], p128[oc8 % 4][:, :],
                                          AF.Identity, bias=bos[:, oc8:oc8 + 1])
                        c3_woev.inc(scalar.drain())
                for dc in range(DC):
                    pi = s * DC + dc
                    scalar.wait_ge(c3_pjmm.sem, pi + 1)
                    if pi >= 2:
                        scalar.wait_ge(c3_x2.sem, pi - 1)
                    scalar.activation(ptmp[pi % 2][:, 0:sw], (pA if dc % 2 == 0 else pB)[:, 0:sw],
                                      AF.Identity, bias=pbs[:, dc:dc + 1])
                    c3_pjev.inc(scalar.drain())
                    scalar.wait_ge(c3_x2.sem, pi + 1)
                    if pi >= 4:
                        scalar.wait_ge(c3_mmst.sem, pi - 3)
                    scalar.activation(x2bf[pi % 4][:, 0:sw], x2c[pi % 2][:, 0:sw], AF.Copy)
                    scalar.drain()
                    scalar.activation(sq3[pi % 4][:, 0:sw], x2c[pi % 2][:, 0:sw], AF.Square)
                    c3_c3.inc(scalar.drain())
                scalar.wait_ge(c3_fin.sem, 4 * s + 2)
                scalar.activation(srow3[:, 4 * 512:4 * 512 + sw], srow3[:, 2 * 512:2 * 512 + sw], AF.Sqrt)
                c3_fin.inc(scalar.drain())

        @blk.tensor
        def _(tensor):
            if not on("s3"): return
            tensor.wait_ge(c3_pre.sem, c3_pre.n)
            for s, (off_t, sw) in enumerate(sup3):
                for oc in range(24):
                    ci = s * 24 + oc
                    if ci >= 2:
                        tensor.wait_ge(c3_cev.sem, ci - 1)
                    if s >= 1 and oc < 2:
                        tensor.wait_ge(c3_pjev.sem, DC * s)
                    if oc == 0:
                        tensor.wait_ge(c3_y2.sem, DC * (s + 1))
                    pd = pA if oc % 2 == 0 else pB
                    for half in range(2):
                        ci2 = ci * 2 + half
                        tensor.wait_ge(c3_cw[ci2 % 2].sem, 16 * (ci2 // 2 + 1))
                        for tap in range(3):
                            for dcl in range(16):
                                dc = 16 * half + dcl
                                mm = tensor.matmul(pd[:, 0:sw], cwt[ci2 % 2][:, tap, dcl, :],
                                                   y2su[:, dc, tap:tap + sw],
                                                   start=(half == 0 and tap == 0 and dcl == 0),
                                                   stop=(half == 1 and tap == 2 and dcl == 15))
                        c3_cmh.inc(mm)
                for b in range(nb_of[s]):
                    gb = blk_gidx(s, b)
                    tensor.wait_ge(c3_cev.sem, 24 * (s + 1))
                    if gb >= 1:
                        tensor.wait_ge(c3_woev.sem, 8 * gb)
                    for m in range(2):
                        base = 0 if m == 0 else 8
                        for hc in range(8):
                            qi = (gb * 2 + m) * 8 + hc
                            pd = p128[hc % 2] if m == 0 else p128[2 + hc % 2]
                            tensor.wait_ge(c3_qw[qi % 6].sem, 16 * (qi // 6 + 1))
                            if qi >= 2:
                                tensor.wait_ge(c3_qkev.sem, qi - 1)
                            for wc in range(8):
                                mm = tensor.matmul(pd[:, :], qksl[qi % 6][:, wc, :],
                                                   csu[:, base + wc, b * 128:(b + 1) * 128],
                                                   start=(wc == 0), stop=(wc == 7))
                            c3_qk.inc(mm)
                    for half in range(2):
                        vi = gb * 2 + half
                        pd = pA if half == 0 else pB
                        if gb >= 1:
                            tensor.wait_ge(c3_vev.sem, vi - 1)
                            tensor.wait_ge(c3_oev.sem, 8 * gb)
                        tensor.wait_ge(c3_vw[vi % 2].sem, 16 * (vi // 2 + 1))
                        for wc in range(8):
                            mm = tensor.matmul(pd[:, :], csu[:, 16 + wc, b * 128:(b + 1) * 128],
                                               vsl[vi % 2][:, wc, :],
                                               start=(wc == 0), stop=(wc == 7))
                        c3_v.inc(mm)
                    tensor.wait_ge(c3_qkev.sem, (gb * 2 + 2) * 8)
                    tensor.wait_ge(c3_vev.sem, 2 * gb + 2)
                    for h in range(16):
                        hi_ = gb * 16 + h
                        hp = (h % 2) * 64
                        if hi_ >= 1:
                            tensor.wait_ge(c3_sm1.sem, hi_)
                        c3_sc.inc(tensor.matmul(p128[h % 2][:, :],
                                                qhsb[hp:hp + 64, h // 2, :],
                                                khsb[hp:hp + 64, h // 2, :],
                                                start=True, stop=True))
                        tensor.wait_ge(c3_p.sem, hi_ + 1)
                        if hi_ >= 1:
                            tensor.wait_ge(c3_ptev.sem, hi_)
                        c3_pt.inc(tensor.matmul(p128[2 + h % 2][:, :], pbf[h % 2][:, :],
                                                ident_bf[:, :], start=True, stop=True))
                        tensor.wait_ge(c3_ptev.sem, hi_ + 1)
                        pd = pA if (h // 2) % 2 == 0 else pB
                        if hi_ >= 4 and h % 2 == 0:
                            tensor.wait_ge(c3_oev.sem, hi_ // 2 - 1)
                        c3_o.inc(tensor.matmul(pd[hp:hp + 64, 0:128],
                                               vtsb[:, h * 64:(h + 1) * 64], ptb[h % 2][:, :],
                                               start=True, stop=True))
                    for oc8 in range(8):
                        wi = gb * 8 + oc8
                        tensor.wait_ge(c3_oev.sem, gb * 8 + 8)
                        if wi >= 4:
                            tensor.wait_ge(c3_woev.sem, wi - 3)
                        tensor.wait_ge(c3_ow[wi % 3].sem, 16 * (wi // 3 + 1))
                        for hc in range(8):
                            mm = tensor.matmul(p128[oc8 % 4][:, :], osl[wi % 3][:, hc, :],
                                               osb[:, hc, :],
                                               start=(hc == 0), stop=(hc == 7))
                        c3_wo.inc(mm)
                def stats3_mm(dc_):
                    pi_ = s * DC + dc_
                    tensor.wait_ge(c3_c3.sem, pi_ + 1)
                    if dc_ == 0 and s >= 1:
                        tensor.wait_ge(c3_fin.sem, 4 * (s - 1) + 1)
                    tensor.matmul(psS3[0][0:1, 0:sw], ones_bf[:, 0:1], x2bf[pi_ % 4][:, 0:sw],
                                  start=(dc_ == 0), stop=(dc_ == DC - 1))
                    mm_ = tensor.matmul(psS3[1][0:1, 0:sw], ones_bf[:, 0:1], sq3[pi_ % 4][:, 0:sw],
                                        start=(dc_ == 0), stop=(dc_ == DC - 1))
                    c3_mmst.inc(mm_)

                for dc in range(DC):
                    pi = s * DC + dc
                    sl = pi % 4
                    tensor.wait_ge(c3_pj[sl].sem, 16 * (pi // 4 + 1))
                    tensor.wait_ge(c3_woev.sem, (cum_nb[s + 1]) * 8)
                    if pi >= 2:
                        tensor.wait_ge(c3_pjev.sem, pi - 1)
                    pd = pA if dc % 2 == 0 else pB
                    for wc in range(8):
                        mm = tensor.matmul(pd[:, 0:sw], pjs[sl][:, wc, :], apre[:, wc, 0:sw],
                                           start=(wc == 0), stop=(wc == 7))
                    c3_pjmm.inc(mm)
                    if dc >= 2:
                        stats3_mm(dc - 2)
                stats3_mm(DC - 2)
                stats3_mm(DC - 1)

    sem_free(_m_s3)

    _m_s3b = sem_mark()
    c3b_ln = Ch(nc, "s3b_ln")
    c3b_x = [Ch(nc, f"s3b_x{j}") for j in range(2)]
    c3b_y = Ch(nc, "s3b_y")
    c3b_w = [Ch(nc, f"s3b_w{j}") for j in range(2)]
    _p3b = []
    for _s, (_o, _sw) in enumerate(sup3):
        _np_ = sum(1 for (toff, tw) in tb if max(_o, toff) < min(_o + _sw, toff + tw))
        _p3b.append(16 * _np_)
    cum3b = []
    _c3 = [0, 0]
    for _s in range(len(sup3)):
        for _dc in range(DC):
            _ki = _s * DC + _dc
            _c3[_ki % 2] += _p3b[_s]
            cum3b.append(_c3[_ki % 2])
    with ExitStack() as stk, nc.Block("s3b") as blk:
        m3b = stk.enter_context(nc.sbuf_tensor("s3b_m3b", [128, 512], F32))
        r3b = stk.enter_context(nc.sbuf_tensor("s3b_r3b", [128, 512], F32))
        xc = [stk.enter_context(nc.sbuf_tensor(f"s3b_xc{j}", [128, 512], F32)) for j in range(2)]
        tc_ = [stk.enter_context(nc.sbuf_tensor(f"s3b_tc{j}", [128, 512], F32)) for j in range(2)]
        yc = [stk.enter_context(nc.sbuf_tensor(f"s3b_yc{j}", [128, 512], BF16)) for j in range(2)]

        @blk.gpsimd
        def _(g):
            if not on("s3b"): return
            for s, (off_t, sw) in enumerate(sup3):
                if s >= 1:
                    g.wait_ge(c3b_y.sem, DC * s)
                g.dma_start(out=m3b[:, 0:sw],
                            in_=ln_rows[2:3, off_t:off_t + sw].broadcast_to([128, sw])).then_inc(c3b_ln.sem, 16)
                g.dma_start(out=r3b[:, 0:sw],
                            in_=ln_rows[3:4, off_t:off_t + sw].broadcast_to([128, sw])).then_inc(c3b_ln.sem, 16)
                c3b_ln.n += 32
                for dc in range(DC):
                    ki = s * DC + dc
                    if ki >= 2:
                        g.wait_ge(c3b_y.sem, ki - 1)
                    g.dma_start(out=xc[ki % 2][:, 0:sw],
                                in_=x2_v[:, dc, off_t:off_t + sw]).then_inc(c3b_x[ki % 2].sem, 16)
                    c3b_x[ki % 2].n += 16

        @blk.vector
        def _(vector):
            if not on("s3b"): return
            for s, (off_t, sw) in enumerate(sup3):
                vector.wait_ge(c3b_ln.sem, 32 * (s + 1))
                for dc in range(DC):
                    ki = s * DC + dc
                    vector.wait_ge(c3b_x[ki % 2].sem, 16 * (ki // 2 + 1))
                    if ki >= 2:
                        vector.wait_ge(c3b_w[ki % 2].sem, cum3b[ki - 2])
                    vector.tensor_tensor(tc_[ki % 2][:, 0:sw], xc[ki % 2][:, 0:sw],
                                         m3b[:, 0:sw], ALU.subtract)
                    vector.drain()
                    c3b_y.inc(vector.tensor_tensor(yc[ki % 2][:, 0:sw], tc_[ki % 2][:, 0:sw],
                                                   r3b[:, 0:sw], ALU.mult))

        @blk.scalar
        def _(scalar):
            if not on("s3b"): return
            for s, (off_t, sw) in enumerate(sup3):
                for dc in range(DC):
                    ki = s * DC + dc
                    scalar.wait_ge(c3b_y.sem, ki + 1)
                    for tt, (toff, tw) in enumerate(tb):
                        lo = max(off_t, toff)
                        hi = min(off_t + sw, toff + tw)
                        if lo >= hi:
                            continue
                        scalar.dma_start(out=y_view(tt, tw)[:, dc, lo - toff:hi - toff],
                                         in_=yc[ki % 2][:, lo - off_t:hi - off_t]
                                         ).then_inc(c3b_w[ki % 2].sem, 16)
                    c3b_w[ki % 2].n = cum3b[ki]
            scalar.wait_ge(c3b_w[0].sem, c3b_w[0].n)
            scalar.wait_ge(c3b_w[1].sem, c3b_w[1].n)

    sem_free(_m_s3b)

    emit_ffn_A("s4a", y_view, tb)
    emit_ffn_B("s4b", x2_v, x3_v, tb, None)

    ns5 = T // 128
    _m_s5 = sem_mark()
    c5_in = [Ch(nc, f"s5_in{j}") for j in range(2)]
    c5_tr = Ch(nc, "s5_tr")
    c5_ev = Ch(nc, "s5_ev")
    c5_w = [Ch(nc, f"s5_w{j}") for j in range(2)]
    with ExitStack() as stk, nc.Block("s5") as blk:
        xin5 = [stk.enter_context(nc.sbuf_tensor(f"s5_xin{j}", [128, DC, 128], F32)) for j in range(2)]
        ob = [stk.enter_context(nc.sbuf_tensor(f"s5_ob{j}", [128, D], F32)) for j in range(2)]
        ps5 = [stk.enter_context(nc.psum_tensor(f"s5_ps{j}", [128, 512], F32)) for j in range(2)]

        @blk.sync
        def _(sync):
            if not on("s5"): return
            for t in range(ns5):
                b = t % 2
                if t >= 2:
                    sync.wait_ge(c5_tr.sem, 8 * (t - 1))
                sync.dma_start(out=xin5[b][:, :, :],
                               in_=x3_v[:, :, t * 128:(t + 1) * 128]).then_inc(c5_in[b].sem, 16)
                c5_in[b].n += 16

        @blk.tensor
        def _(tensor):
            if not on("s5"): return
            for t in range(ns5):
                b = t % 2
                tensor.wait_ge(c5_in[b].sem, 16 * (t // 2 + 1))
                for r in range(8):
                    R = 8 * t + r
                    if R >= 2:
                        tensor.wait_ge(c5_ev.sem, R - 1)
                    for k in range(4):
                        dc = r * 4 + k
                        mm = tensor.matmul(ps5[R % 2][:, k * 128:(k + 1) * 128],
                                           xin5[b][:, dc, :], ident_f[:, :],
                                           start=True, stop=True)
                    c5_tr.inc(mm)

        @blk.vector
        def _(vector):
            if not on("s5"): return
            for t in range(ns5):
                b = t % 2
                for r in range(8):
                    R = 8 * t + r
                    vector.wait_ge(c5_tr.sem, R + 1)
                    if t >= 2 and r == 0:
                        vector.wait_ge(c5_w[b].sem, 16 * (t // 2))
                    c5_ev.inc(vector.tensor_copy(ob[b][:, r * 512:(r + 1) * 512], ps5[R % 2][:, :]))

        @blk.gpsimd
        def _(g):
            if not on("s5"): return
            for t in range(ns5):
                b = t % 2
                g.wait_ge(c5_ev.sem, 8 * (t + 1))
                g.dma_start(out=out[t * 128:(t + 1) * 128, :], in_=ob[b][:, :]).then_inc(c5_w[b].sem, 16)
                c5_w[b].n += 16
            g.wait_ge(c5_w[0].sem, c5_w[0].n)
            g.wait_ge(c5_w[1].sem, c5_w[1].n)

    sem_free(_m_s5)

    c_dbg = Ch(nc, "dbg")
    with nc.Block("finish") as blk:
        @blk.sync
        def _(sync):
            for name, (srct, o, shape) in dbg.items():
                flat = " ".join(chr(97 + i) for i in range(len(shape)))
                pat = f"{flat} -> ({flat})"
                sync.dma_start(out=o.rearrange(pat), in_=srct.rearrange(pat)).then_inc(c_dbg.sem, 16)
                c_dbg.n += 16
            sync.wait_ge(c_dbg.sem, c_dbg.n)

    return nc, dbg


def core_inputs(inp, core, cfg: Cfg):
    b, half = core // 2, core % 2
    x_full = inp["L0_attn_inputs"][b]
    T = cfg.T
    own = np.ascontiguousarray(x_full[half * 2048:half * 2048 + T])
    hl = (x_full[half * 2048 - 1:half * 2048] if half == 1
          else np.zeros((1, D), np.float32))
    nxt = half * 2048 + T
    hr = (x_full[nxt:nxt + 1] if (half == 0 and nxt < x_full.shape[0])
          else np.zeros((1, D), np.float32))
    m = {
        "x_own": own, "x_hl": np.ascontiguousarray(hl), "x_hr": np.ascontiguousarray(hr),
        "w_conv": inp["L0_attn_conv_w"], "b_conv": inp["L0_attn_conv_b"],
        "w_q": inp["L0_attn_wq"].reshape(WQK, NHEAD * DH),
        "b_q": inp["L0_attn_bq"].reshape(NHEAD * DH),
        "w_k": inp["L0_attn_wk"].reshape(WQK, NHEAD * DH),
        "b_k": inp["L0_attn_bk"].reshape(NHEAD * DH),
        "w_v": inp["L0_attn_wv"].reshape(WQK, NHEAD * DH),
        "b_v": inp["L0_attn_bv"].reshape(NHEAD * DH),
        "w_o": inp["L0_attn_wo"].reshape(NHEAD * DH, WQK),
        "b_o": inp["L0_attn_bo"],
        "w_p": inp["L0_attn_proj_w"], "b_p": inp["L0_attn_proj_b"],
        "gq": inp["L0_ffn_gate_Wq"][:cfg.I], "uq": inp["L0_ffn_up_Wq"][:cfg.I],
        "dq": inp["L0_ffn_down_Wq"][:, :cfg.I // 2],
        "gsc": inp["L0_ffn_gate_scale"][:cfg.I], "gzr": inp["L0_ffn_gate_zero"][:cfg.I],
        "usc": inp["L0_ffn_up_scale"][:cfg.I], "uzr": inp["L0_ffn_up_zero"][:cfg.I],
        "dsc": inp["L0_ffn_down_scale"][:, :cfg.I // 64],
        "dzr": inp["L0_ffn_down_zero"][:, :cfg.I // 64],
    }
    return {k: np.ascontiguousarray(v) for k, v in m.items()}


from concourse.bass_utils import run_bass_kernel_spmd


def kernel(**inputs):
    cfg = Cfg()
    nc, _ = build_program(cfg)
    in_maps = [core_inputs(inputs, c, cfg) for c in range(8)]
    res = run_bass_kernel_spmd(nc, in_maps, core_ids=list(range(8)))
    out = np.empty((4, 4096, 4096), np.float32)
    for c in range(8):
        b, half = c // 2, c % 2
        out[b, half * 2048:(half + 1) * 2048] = np.asarray(res.results[c]["out"])
    return out


# version 04 (final)
# speedup vs baseline: 1.1556x; 1.1556x over previous
import math
from contextlib import ExitStack

import numpy as np
import concourse.bass as bass
import concourse.mybir as mybir

F32 = mybir.dt.float32
BF16 = mybir.dt.bfloat16
I32 = mybir.dt.int32
AF = mybir.ActivationFunctionType
ALU = mybir.AluOpType
AX = mybir.AxisListType

D = 4096
DC = 32
CW = 3072
NHEAD, DH = 16, 64
WQK = 1024


class Cfg:
    def __init__(self, T=2048, n_ic=86, debug=(),
                 stages=("s0", "s1a", "s1b", "s2", "s3", "s3b", "s4a", "s4b", "s5")):
        self.T = T
        self.TC = T + 2
        self.n_ic = n_ic
        self.I = n_ic * 128
        self.IBC = 2
        self.debug = set(debug)
        self.stages = set(stages)
        n_ta = math.ceil(self.TC / 410)
        w = math.ceil(self.TC / n_ta)
        self.ta = []
        c = 0
        while c < self.TC:
            ww = min(w, self.TC - c)
            self.ta.append((c, ww))
            c += ww
        self.s0 = []
        c = 0
        while c + 128 < self.TC:
            self.s0.append(c)
            c += 128
        self.s0.append(self.TC - 128)
        assert T % 128 == 0 and T >= 256


def ceil_div(a, b):
    return -(-a // b)


class Ch:
    live = []

    def __init__(self, nc, name):
        self.sem = nc.alloc_semaphore(name)
        self.n = 0
        Ch.live.append(self)

    def inc(self, instr, k=1):
        instr.then_inc(self.sem, k)
        self.n += k


def build_program(cfg: Cfg):
    nc = bass.Bass("TRN2", target_bir_lowering=False)
    Ch.live = []

    def sem_mark():
        return len(Ch.live)

    def sem_free(mark):
        nc.clear_and_free_semaphores([c.sem for c in Ch.live[mark:]])
        del Ch.live[mark:]
    T, TC, n_ic, I_ = cfg.T, cfg.TC, cfg.n_ic, cfg.I
    on = lambda s: s in cfg.stages

    x_own = nc.dram_tensor("x_own", [T, D], F32, kind="ExternalInput").ap()
    x_hl = nc.dram_tensor("x_hl", [1, D], F32, kind="ExternalInput").ap()
    x_hr = nc.dram_tensor("x_hr", [1, D], F32, kind="ExternalInput").ap()
    w_conv = nc.dram_tensor("w_conv", [3, D, CW], F32, kind="ExternalInput").ap()
    b_conv = nc.dram_tensor("b_conv", [CW], F32, kind="ExternalInput").ap()
    w_q = nc.dram_tensor("w_q", [WQK, NHEAD * DH], F32, kind="ExternalInput").ap()
    b_q = nc.dram_tensor("b_q", [NHEAD * DH], F32, kind="ExternalInput").ap()
    w_k = nc.dram_tensor("w_k", [WQK, NHEAD * DH], F32, kind="ExternalInput").ap()
    b_k = nc.dram_tensor("b_k", [NHEAD * DH], F32, kind="ExternalInput").ap()
    w_v = nc.dram_tensor("w_v", [WQK, NHEAD * DH], F32, kind="ExternalInput").ap()
    b_v = nc.dram_tensor("b_v", [NHEAD * DH], F32, kind="ExternalInput").ap()
    w_o = nc.dram_tensor("w_o", [NHEAD * DH, WQK], F32, kind="ExternalInput").ap()
    b_o = nc.dram_tensor("b_o", [WQK], F32, kind="ExternalInput").ap()
    w_p = nc.dram_tensor("w_p", [WQK, D], F32, kind="ExternalInput").ap()
    b_p = nc.dram_tensor("b_p", [D], F32, kind="ExternalInput").ap()
    gq = nc.dram_tensor("gq", [I_, D // 2], I32, kind="ExternalInput").ap()
    uq = nc.dram_tensor("uq", [I_, D // 2], I32, kind="ExternalInput").ap()
    dq = nc.dram_tensor("dq", [D, I_ // 2], I32, kind="ExternalInput").ap()
    gsc = nc.dram_tensor("gsc", [I_, D // 64], F32, kind="ExternalInput").ap()
    gzr = nc.dram_tensor("gzr", [I_, D // 64], F32, kind="ExternalInput").ap()
    usc = nc.dram_tensor("usc", [I_, D // 64], F32, kind="ExternalInput").ap()
    uzr = nc.dram_tensor("uzr", [I_, D // 64], F32, kind="ExternalInput").ap()
    dsc = nc.dram_tensor("dsc", [D, I_ // 64], F32, kind="ExternalInput").ap()
    dzr = nc.dram_tensor("dzr", [D, I_ // 64], F32, kind="ExternalInput").ap()

    out = nc.dram_tensor("out", [T, D], F32, kind="ExternalOutput").ap()

    x_fm = nc.dram_tensor("x_fm", [D, TC], F32).ap()
    stage_d = nc.dram_tensor("stage_d", [D, I_], BF16).ap()
    stage_dT = nc.dram_tensor("stage_dT", [I_, D], BF16).ap()
    NBLK_A = n_ic // 2
    gT_flat = nc.dram_tensor("gT_flat", [NBLK_A * 128 * DC * 256], BF16).ap()
    uT_flat = nc.dram_tensor("uT_flat", [NBLK_A * 128 * DC * 256], BF16).ap()
    h_flat = nc.dram_tensor("h_flat", [5 * 128 * n_ic * 512], BF16).ap()
    y_flat = nc.dram_tensor("y_flat", [5 * 128 * DC * 512], BF16).ap()

    def gT_view(t, b_):
        base = b_ * 128 * DC * 256
        return t[base:base + 128 * DC * 256].rearrange("(p c i) -> p c i", p=128, c=DC)

    def h_view(tt, w):
        base = tt * 128 * n_ic * 512
        return h_flat[base:base + 128 * n_ic * w].rearrange("(p c w) -> p c w", p=128, c=n_ic)

    def y_view(tt, w):
        base = tt * 128 * DC * 512
        return y_flat[base:base + 128 * DC * w].rearrange("(p c w) -> p c w", p=128, c=DC)
    xmid_fm = nc.dram_tensor("xmid_fm", [D, TC], F32).ap()
    x2_fm = nc.dram_tensor("x2_fm", [D, T], F32).ap()
    x3_fm = nc.dram_tensor("x3_fm", [D, T], F32).ap()
    convst = nc.dram_tensor("convst", [3, D, CW], BF16).ap()
    convT = nc.dram_tensor("convT", [24 * 2 * 128 * 3 * 16 * 128], BF16).ap()
    qkvT = nc.dram_tensor("qkvT", [24 * 128 * 8 * 128], BF16).ap()
    vT = nc.dram_tensor("vT", [2 * 128 * 8 * 512], BF16).ap()
    projT = nc.dram_tensor("projT", [32 * 128 * 8 * 128], BF16).ap()

    def convT_view(oc, half):
        base = (oc * 2 + half) * 128 * 3 * 16 * 128
        return convT[base:base + 128 * 3 * 16 * 128].rearrange("(p t c o) -> p t c o", p=128, t=3, c=16)

    def qkvT_view(idx):
        base = idx * 128 * 8 * 128
        return qkvT[base:base + 128 * 8 * 128].rearrange("(p w o) -> p w o", p=128, w=8)

    def vT_view(half):
        base = half * 128 * 8 * 512
        return vT[base:base + 128 * 8 * 512].rearrange("(p w o) -> p w o", p=128, w=8)

    def projT_view(dc):
        base = dc * 128 * 8 * 128
        return projT[base:base + 128 * 8 * 128].rearrange("(p w o) -> p w o", p=128, w=8)
    projst = nc.dram_tensor("projst", [WQK, D], BF16).ap()
    qkvst = nc.dram_tensor("qkvst", [4, WQK, WQK], BF16).ap()
    ln_rows = nc.dram_tensor("ln_rows", [4, TC], F32).ap()

    dbg = {}
    def dbg_dram(name):
        src = {"x_fm": (x_fm, [D, TC], F32), "y_flat": (y_flat, [5 * 128 * DC * 512], BF16),
               "gT_flat": (gT_flat, [NBLK_A * 128 * DC * 256], BF16),
               "uT_flat": (uT_flat, [NBLK_A * 128 * DC * 256], BF16),
               "stage_d": (stage_d, [D, I_], BF16), "h_flat": (h_flat, [5 * 128 * n_ic * 512], BF16),
               "xmid_fm": (xmid_fm, [D, TC], F32), "x2_fm": (x2_fm, [D, T], F32),
               "x3_fm": (x3_fm, [D, T], F32),
               "convst": (convst, [3, D, CW], BF16), "projst": (projst, [WQK, D], BF16),
               "ln_rows": (ln_rows, [4, TC], F32)}
        t, shape, dt = src[name]
        o = nc.dram_tensor("dbg_" + name, shape, dt, kind="ExternalOutput").ap()
        dbg[name] = (t, o, shape)

    for name in cfg.debug:
        dbg_dram(name)

    x_fm_v = x_fm.rearrange("(c p) t -> p c t", p=128)
    xmid_v = xmid_fm.rearrange("(c p) t -> p c t", p=128)
    x2_v = x2_fm.rearrange("(c p) t -> p c t", p=128)
    x3_v = x3_fm.rearrange("(c p) t -> p c t", p=128)

    ident_bf = nc.alloc_sbuf_tensor("ident_bf", [128, 128], BF16)
    ident_f = nc.alloc_sbuf_tensor("ident_f", [128, 128], F32)
    mask = nc.alloc_sbuf_tensor("mask", [128, 128], F32)
    ones_bf = nc.alloc_sbuf_tensor("ones_bf", [128, 2], BF16)
    ones_f = nc.alloc_sbuf_tensor("ones_f", [128, 2], F32)

    with nc.Block("init") as blk:
        @blk.gpsimd
        def _(g):
            g.memset(ones_bf[:, :], 1.0)
            g.memset(ones_f[:, :], 1.0)
            g.memset(mask[:, :], 0.0)
            g.drain()
            g.affine_select(ident_bf[:, :], ones_bf[:, 0:1].broadcast_to([128, 128]),
                            [[-1, 128]], ALU.is_equal, 0.0, base=0, channel_multiplier=1)
            g.affine_select(ident_f[:, :], ones_f[:, 0:1].broadcast_to([128, 128]),
                            [[-1, 128]], ALU.is_equal, 0.0, base=0, channel_multiplier=1)
            g.affine_select(mask[:, :], mask[:, :], [[-1, 128]], ALU.is_ge, -1e9,
                            base=0, channel_multiplier=1)

    ns0 = len(cfg.s0)
    _m_s0 = sem_mark()
    c_in = [Ch(nc, "s0_in0"), Ch(nc, "s0_in1")]
    c_sq = Ch(nc, "s0_sq")
    c_var = Ch(nc, "s0_var")
    c_srt = Ch(nc, "s0_srt")
    c_y = Ch(nc, "s0_y")
    c_tr = Ch(nc, "s0_tr")
    c_ev = Ch(nc, "s0_ev")
    c_out = [Ch(nc, "s0_out0"), Ch(nc, "s0_out1")]

    s0_stk = ExitStack()
    _out_cnt = []
    for _t in range(ns0):
        _off = cfg.s0[_t]
        _n = sum(1 for (toff, tw) in cfg.ta if max(_off, toff) < min(_off + 128, toff + tw))
        _out_cnt.append(16 * (_n + 1))
    cum_out = []
    _c = [0, 0]
    for _t in range(ns0):
        _c[_t % 2] += _out_cnt[_t]
        cum_out.append(_c[_t % 2])
    with s0_stk as stk, nc.Block("s0") as blk:
        xin = [stk.enter_context(nc.sbuf_tensor(f"s0_xin{j}", [128, D], F32)) for j in range(2)]
        sq = stk.enter_context(nc.sbuf_tensor("s0_sq_t", [128, D], F32))
        ytm = [stk.enter_context(nc.sbuf_tensor(f"s0_ytm{j}", [128, D], BF16)) for j in range(2)]
        st_ = stk.enter_context(nc.sbuf_tensor("s0_stats", [128, 16], F32))
        yout = [stk.enter_context(nc.sbuf_tensor(f"s0_yout{j}", [128, D], BF16)) for j in range(2)]
        xout = [stk.enter_context(nc.sbuf_tensor(f"s0_xout{j}", [128, D], F32)) for j in range(2)]
        ps = [stk.enter_context(nc.psum_tensor(f"s0_ps{j}", [128, 512], F32)) for j in range(2)]

        @blk.sync
        def _(sync):
            if not on("s0"): return
            for t in range(ns0):
                b = t % 2
                off = cfg.s0[t]
                lo, hi = off - 1, off + 126
                if t >= 2:
                    sync.wait_ge(c_tr.sem, 16 * (t - 1))
                if lo < 0:
                    sync.dma_start(out=xin[b][0:1, :], in_=x_hl).then_inc(c_in[b].sem, 16)
                    sync.dma_start(out=xin[b][1:128, :], in_=x_own[0:127, :]).then_inc(c_in[b].sem, 16)
                elif hi >= T:
                    sync.dma_start(out=xin[b][0:127, :], in_=x_own[lo:lo + 127, :]).then_inc(c_in[b].sem, 16)
                    sync.dma_start(out=xin[b][127:128, :], in_=x_hr).then_inc(c_in[b].sem, 16)
                else:
                    sync.dma_start(out=xin[b][0:64, :], in_=x_own[lo:lo + 64, :]).then_inc(c_in[b].sem, 16)
                    sync.dma_start(out=xin[b][64:128, :], in_=x_own[lo + 64:lo + 128, :]).then_inc(c_in[b].sem, 16)
                c_in[b].n += 32

        @blk.scalar
        def _(scalar):
            if not on("s0"): return
            for t in range(ns0):
                b = t % 2
                s = 8 * (t % 2)
                scalar.wait_ge(c_in[b].sem, 32 * (t // 2 + 1))
                if t >= 1:
                    scalar.wait_ge(c_var.sem, t)
                scalar.activation(sq[:, :], xin[b][:, :], AF.Square)
                c_sq.inc(scalar.drain())
                scalar.wait_ge(c_var.sem, t + 1)
                scalar.activation(st_[:, s + 6:s + 7], st_[:, s + 5:s + 6], AF.Sqrt)
                c_srt.inc(scalar.drain())

        @blk.vector
        def _(vector):
            if not on("s0"): return
            for t in range(ns0):
                b = t % 2
                s = 8 * (t % 2)
                vector.wait_ge(c_in[b].sem, 32 * (t // 2 + 1))
                vector.wait_ge(c_sq.sem, t + 1)
                vector.reduce_sum(st_[:, s + 0:s + 1], xin[b][:, :], axis=AX.X)
                vector.reduce_sum(st_[:, s + 1:s + 2], sq[:, :], axis=AX.X)
                vector.drain()
                vector.tensor_scalar(st_[:, s + 2:s + 3], st_[:, s + 0:s + 1], 1.0 / D, None, ALU.mult)
                vector.tensor_scalar(st_[:, s + 3:s + 4], st_[:, s + 1:s + 2], 1.0 / D, None, ALU.mult)
                vector.drain()
                vector.tensor_tensor(st_[:, s + 4:s + 5], st_[:, s + 2:s + 3], st_[:, s + 2:s + 3], ALU.mult)
                vector.drain()
                vector.tensor_tensor(st_[:, s + 5:s + 6], st_[:, s + 3:s + 4], st_[:, s + 4:s + 5], ALU.subtract)
                vector.drain()
                c_var.inc(vector.tensor_scalar(st_[:, s + 5:s + 6], st_[:, s + 5:s + 6], 1e-5, None, ALU.add))
                vector.wait_ge(c_srt.sem, t + 1)
                vector.reciprocal(st_[:, s + 7:s + 8], st_[:, s + 6:s + 7])
                vector.drain()
                if t >= 2:
                    vector.wait_ge(c_tr.sem, 16 * (t - 2) + 8)
                c_y.inc(vector.tensor_scalar(ytm[b][:, :], xin[b][:, :], st_[:, s + 2:s + 3], st_[:, s + 7:s + 8],
                                             ALU.subtract, ALU.mult))
                for r in range(16):
                    R = 16 * t + r
                    vector.wait_ge(c_tr.sem, R + 1)
                    if t >= 2 and r == 0:
                        vector.wait_ge(c_out[b].sem, cum_out[t - 2])
                    if r < 8:
                        c_ev.inc(vector.tensor_copy(yout[b][:, r * 512:(r + 1) * 512], ps[R % 2][:, :]))
                    else:
                        c_ev.inc(vector.tensor_copy(xout[b][:, (r - 8) * 512:(r - 7) * 512], ps[R % 2][:, :]))

        @blk.tensor
        def _(tensor):
            if not on("s0"): return
            for t in range(ns0):
                b = t % 2
                tensor.wait_ge(c_y.sem, t + 1)
                for r in range(16):
                    R = 16 * t + r
                    if R >= 2:
                        tensor.wait_ge(c_ev.sem, R - 1)
                    for k in range(4):
                        c = (r % 8) * 4 + k
                        if r < 8:
                            mm = tensor.matmul(ps[R % 2][:, k * 128:(k + 1) * 128],
                                               ytm[b][:, c * 128:(c + 1) * 128], ident_bf[:, :],
                                               start=True, stop=True)
                        else:
                            mm = tensor.matmul(ps[R % 2][:, k * 128:(k + 1) * 128],
                                               xin[b][:, c * 128:(c + 1) * 128], ident_f[:, :],
                                               start=True, stop=True)
                        if k == 3:
                            c_tr.inc(mm)

        @blk.gpsimd
        def _(g):
            if not on("s0"): return
            for t in range(ns0):
                b = t % 2
                off = cfg.s0[t]
                g.wait_ge(c_ev.sem, 16 * t + 8)
                if t == ns0 - 1 and t >= 1:
                    g.wait_ge(c_out[(t - 1) % 2].sem, cum_out[t - 1])
                for tt, (toff, tw) in enumerate(cfg.ta):
                    lo = max(off, toff)
                    hi = min(off + 128, toff + tw)
                    if lo >= hi:
                        continue
                    g.dma_start(out=y_view(tt, tw)[:, :, lo - toff:hi - toff],
                                in_=yout[b][:, :].rearrange("p (c t) -> p c t", t=128)[:, :, lo - off:hi - off]
                                ).then_inc(c_out[b].sem, 16)
                g.wait_ge(c_ev.sem, 16 * (t + 1))
                g.dma_start(out=x_fm_v[:, :, off:off + 128],
                            in_=xout[b][:, :].rearrange("p (c t) -> p c t", t=128)).then_inc(c_out[b].sem, 16)
                c_out[b].n = cum_out[t]
            g.wait_ge(c_out[0].sem, c_out[0].n)
            g.wait_ge(c_out[1].sem, c_out[1].n)

    sem_free(_m_s0)

    _m_s1a = sem_mark()
    deq_iter = ([("g", c) for c in range(n_ic)] + [("u", c) for c in range(n_ic)]
                + [("d", c) for c in range(DC)])
    deq_src = {"g": (gq, gsc, gzr, None, D // 2, D // 64),
               "u": (uq, usc, uzr, None, D // 2, D // 64),
               "d": (dq, dsc, dzr, stage_d, I_ // 2, I_ // 64)}
    c_qin = [Ch(nc, "s1a_qin0"), Ch(nc, "s1a_qin1")]
    c_deq = Ch(nc, "s1a_deq")
    c_stw = [Ch(nc, "s1a_stw0"), Ch(nc, "s1a_stw1")]
    c_tr1 = Ch(nc, "s1a_tr1")
    c_tev = Ch(nc, "s1a_tev")
    c_gtw = Ch(nc, "s1a_gtw")
    c_cin = [Ch(nc, "s1a_cin0"), Ch(nc, "s1a_cin1")]
    c_cast = Ch(nc, "s1a_cast")
    c_cw = [Ch(nc, "s1a_cw0"), Ch(nc, "s1a_cw1")]
    cast_iter = ([("c", w, c, hh) for w in range(3) for c in range(DC) for hh in range(2)]
                 + [("p", 0, c, hh) for c in range(WQK // 128) for hh in range(2)]
                 + [("4", m, c, hh) for m in range(4) for c in range(WQK // 128) for hh in range(2)])
    KMAX = max(D // 2, I_ // 2)
    NGMAX = max(D // 64, I_ // 64)
    with ExitStack() as stk, nc.Block("s1a") as blk:
        qin = [stk.enter_context(nc.sbuf_tensor(f"s1a_qin{j}", [128, KMAX], I32)) for j in range(2)]
        scz = [stk.enter_context(nc.sbuf_tensor(f"s1a_scz{j}", [128, 2 * NGMAX], F32)) for j in range(2)]
        ei = stk.enter_context(nc.sbuf_tensor("s1a_ei", [128, KMAX], I32))
        tmpf = stk.enter_context(nc.sbuf_tensor("s1a_tmpf", [128, KMAX], F32))
        stout = [stk.enter_context(nc.sbuf_tensor(f"s1a_stout{j}", [128, 2 * KMAX], BF16)) for j in range(2)]
        wTstg = stk.enter_context(nc.sbuf_tensor("s1a_wTstg", [128, DC, 256], BF16))
        wTstu = stk.enter_context(nc.sbuf_tensor("s1a_wTstu", [128, DC, 256], BF16))
        psT = [stk.enter_context(nc.psum_tensor(f"s1a_psT{j}", [128, 512], F32)) for j in range(2)]
        cin = [stk.enter_context(nc.sbuf_tensor(f"s1a_cin{j}", [128, max(CW, D) // 2], F32)) for j in range(2)]
        cout = [stk.enter_context(nc.sbuf_tensor(f"s1a_cout{j}", [128, max(CW, D) // 2], BF16)) for j in range(2)]

        @blk.sync
        def _(sync):
            if not on("s1a"): return
            for it, (m, c) in enumerate(deq_iter):
                sl = it % 2
                src, ssc, szr, _, K, NG = deq_src[m]
                if it >= 2:
                    sync.wait_ge(c_deq.sem, it - 1)
                sync.dma_start(out=qin[sl][:, 0:K],
                               in_=src.rearrange("(c p) k -> p c k", p=128)[:, c, :]).then_inc(c_qin[sl].sem, 16)
                sync.dma_start(out=scz[sl][:, 0:NG],
                               in_=ssc.rearrange("(c p) k -> p c k", p=128)[:, c, :]).then_inc(c_qin[sl].sem, 16)
                sync.dma_start(out=scz[sl][:, NG:2 * NG],
                               in_=szr.rearrange("(c p) k -> p c k", p=128)[:, c, :]).then_inc(c_qin[sl].sem, 16)
                c_qin[sl].n += 48

        @blk.vector
        def _(vector):
            if not on("s1a"): return
            for it, (m, c) in enumerate(deq_iter):
                sl = it % 2
                _, _, _, _, K, NG = deq_src[m]
                vector.wait_ge(c_qin[sl].sem, 48 * (it // 2 + 1))
                if it >= 2:
                    pm, _pc = deq_iter[it - 2]
                    if pm == "d":
                        j0 = (it - 2) - 2 * n_ic
                        vector.wait_ge(c_stw[sl].sem, 16 * (j0 // 2 + 1))
                    else:
                        vector.wait_ge(c_tr1.sem, 8 * (it - 1))
                sc_b = scz[sl][:, 0:NG].unsqueeze(2).broadcast_to([128, NG, 32])
                zr_b = scz[sl][:, NG:2 * NG].unsqueeze(2).broadcast_to([128, NG, 32])
                ev = stout[sl][:, 0:2 * K].rearrange("p (j two) -> p two j", two=2)
                qv = qin[sl][:, 0:K].rearrange("p (g r) -> p g r", r=32)
                tv = tmpf[:, 0:K].rearrange("p (g r) -> p g r", r=32)
                ev2 = ev[:, 0, :].rearrange("p (g r) -> p g r", r=32)
                od2 = ev[:, 1, :].rearrange("p (g r) -> p g r", r=32)
                vector.tensor_scalar(ei[:, 0:K], qin[sl][:, 0:K], 4, 15,
                                     ALU.logical_shift_right, ALU.bitwise_and)
                vector.drain()
                vector.tensor_tensor(tv, ei[:, 0:K].rearrange("p (g r) -> p g r", r=32), zr_b, ALU.subtract)
                vector.drain()
                vector.tensor_tensor(ev2, tv, sc_b, ALU.mult)
                vector.drain()
                vector.tensor_single_scalar(ei[:, 0:K], qin[sl][:, 0:K], 15, ALU.bitwise_and)
                vector.drain()
                vector.tensor_tensor(tv, ei[:, 0:K].rearrange("p (g r) -> p g r", r=32), zr_b, ALU.subtract)
                vector.drain()
                c_deq.inc(vector.tensor_tensor(od2, tv, sc_b, ALU.mult))

        @blk.scalar
        def _(scalar):
            if not on("s1a"): return
            for it, (m, c) in enumerate(deq_iter):
                sl = it % 2
                _, _, _, stg, K, NG = deq_src[m]
                if m == "d":
                    scalar.wait_ge(c_deq.sem, it + 1)
                    scalar.dma_start(out=stg.rearrange("(c p) k -> p c k", p=128)[:, c, :],
                                     in_=stout[sl][:, 0:2 * K]).then_inc(c_stw[sl].sem, 16)
                    c_stw[sl].n += 16
                else:
                    wdst = wTstg if m == "g" else wTstu
                    if c % 2 == 0 and c >= 2:
                        nw = sum(1 for (m2, c2) in deq_iter[:it] if m2 != "d" and c2 % 2 == 1)
                        scalar.wait_ge(c_gtw.sem, 16 * nw)
                    for grp in range(8):
                        gidx = it * 8 + grp
                        scalar.wait_ge(c_tr1.sem, gidx + 1)
                        scalar.activation(
                            wdst[:, grp * 4:grp * 4 + 4, (c % 2) * 128:(c % 2) * 128 + 128],
                            psT[gidx % 2][:, :], AF.Copy)
                        c_tev.inc(scalar.drain())
                    if c % 2 == 1:
                        t_ = gT_flat if m == "g" else uT_flat
                        scalar.dma_start(out=gT_view(t_, c // 2),
                                         in_=wdst[:, :, :]).then_inc(c_gtw.sem, 16)
                        c_gtw.n += 16
            for it, (m, w, c, hh) in enumerate(cast_iter):
                sl = it % 2
                scalar.wait_ge(c_cin[sl].sem, 16 * (it // 2 + 1))
                if it >= 2:
                    scalar.wait_ge(c_cw[sl].sem, 16 * (it // 2))
                n = (CW if m == "c" else (D if m == "p" else WQK)) // 2
                scalar.activation(cout[sl][:, 0:n], cin[sl][:, 0:n], AF.Copy)
                c_cast.inc(scalar.drain())
                if m == "c":
                    o = convst.rearrange("w (c p) o -> p w c o", p=128)[:, w, c, hh * n:(hh + 1) * n]
                elif m == "p":
                    o = projst.rearrange("(c p) k -> p c k", p=128)[:, c, hh * n:(hh + 1) * n]
                else:
                    o = qkvst.rearrange("m (c p) k -> p m c k", p=128)[:, w, c, hh * n:(hh + 1) * n]
                scalar.dma_start(out=o, in_=cout[sl][:, 0:n]).then_inc(c_cw[sl].sem, 16)
                c_cw[sl].n += 16

        @blk.tensor
        def _(tensor):
            if not on("s1a"): return
            for it, (m, c) in enumerate(deq_iter):
                if m == "d":
                    continue
                sl = it % 2
                tensor.wait_ge(c_deq.sem, it + 1)
                for grp in range(8):
                    gidx = it * 8 + grp
                    if gidx >= 2:
                        tensor.wait_ge(c_tev.sem, gidx - 1)
                    for k in range(4):
                        dc = grp * 4 + k
                        mm = tensor.matmul(psT[gidx % 2][:, k * 128:(k + 1) * 128],
                                           stout[sl][:, dc * 128:(dc + 1) * 128],
                                           ident_bf[:, :], start=True, stop=True)
                    c_tr1.inc(mm)

        @blk.gpsimd
        def _(g):
            if not on("s1a"): return
            for it, (m, w, c, hh) in enumerate(cast_iter):
                sl = it % 2
                if it >= 2:
                    g.wait_ge(c_cast.sem, it - 1)
                if m == "c":
                    n = CW // 2
                    src = w_conv.rearrange("w (c p) o -> p w c o", p=128)[:, w, c, hh * n:(hh + 1) * n]
                elif m == "p":
                    n = D // 2
                    src = w_p.rearrange("(c p) k -> p c k", p=128)[:, c, hh * n:(hh + 1) * n]
                else:
                    n = WQK // 2
                    src = [w_q, w_k, w_v, w_o][w].rearrange("(c p) k -> p c k", p=128)[:, c, hh * n:(hh + 1) * n]
                g.dma_start(out=cin[sl][:, 0:n], in_=src).then_inc(c_cin[sl].sem, 16)
                c_cin[sl].n += 16
            g.wait_ge(c_stw[0].sem, c_stw[0].n)
            g.wait_ge(c_stw[1].sem, c_stw[1].n)
            g.wait_ge(c_gtw.sem, c_gtw.n)
            g.wait_ge(c_cw[0].sem, c_cw[0].n)
            g.wait_ge(c_cw[1].sem, c_cw[1].n)

    sem_free(_m_s1a)

    assert n_ic % cfg.IBC == 0
    nblk = n_ic // cfg.IBC
    IBC = cfg.IBC

    def emit_ffn_A(tag, yv_fn, ta_list, extra_gp=None, extra_sc=None):
        _m = sem_mark()
        n_ta_ = len(ta_list)
        TAW_ = max(w for _, w in ta_list)
        c_wt = [Ch(nc, f"{tag}_wt{j}") for j in range(2)]
        c_yb = [Ch(nc, f"{tag}_yb{j}") for j in range(2)]
        c_mm = Ch(nc, f"{tag}_mm")
        c_sg = Ch(nc, f"{tag}_sg")
        c_hv = Ch(nc, f"{tag}_hv")
        c_hw = [Ch(nc, f"{tag}_hw{j}") for j in range(2)]

        def mm_after_tt(gt):
            return 2 * IBC * gt

        def pair_idx(b_, tt, ic):
            return (b_ * n_ta_ + tt) * IBC + ic

        with ExitStack() as stk, nc.Block(tag) as blk:
            wTg = [stk.enter_context(nc.sbuf_tensor(f"{tag}_wTg{j}", [128, DC, IBC * 128], BF16)) for j in range(2)]
            wTu = [stk.enter_context(nc.sbuf_tensor(f"{tag}_wTu{j}", [128, DC, IBC * 128], BF16)) for j in range(2)]
            ybuf = [stk.enter_context(nc.sbuf_tensor(f"{tag}_yb{j}", [128, DC, TAW_], BF16)) for j in range(2)]
            sgb = [stk.enter_context(nc.sbuf_tensor(f"{tag}_sg{j}", [128, TAW_], F32)) for j in range(2)]
            hbuf = [stk.enter_context(nc.sbuf_tensor(f"{tag}_hb{j}", [128, TAW_], BF16)) for j in range(2)]
            psgu = [stk.enter_context(nc.psum_tensor(f"{tag}_ps{j}", [128, TAW_], F32)) for j in range(4)]

            @blk.sync
            def _(sync):
                if not on(tag): return
                for b_ in range(nblk):
                    sl = b_ % 2
                    if b_ >= 2:
                        sync.wait_ge(c_mm.sem, 2 * IBC * n_ta_ * (b_ - 1))
                    sync.dma_start(out=wTg[sl][:, :, :],
                                   in_=gT_view(gT_flat, b_)).then_inc(c_wt[sl].sem, 16)
                    sync.dma_start(out=wTu[sl][:, :, :],
                                   in_=gT_view(uT_flat, b_)).then_inc(c_wt[sl].sem, 16)
                    c_wt[sl].n += 32
                    for tt in range(n_ta_):
                        gt = b_ * n_ta_ + tt
                        ysl = gt % 2
                        if gt >= 2:
                            sync.wait_ge(c_mm.sem, mm_after_tt(gt - 1))
                        off, w = ta_list[tt]
                        sync.dma_start(out=ybuf[ysl][:, :, 0:w],
                                       in_=yv_fn(tt, w)).then_inc(c_yb[ysl].sem, 16)
                        c_yb[ysl].n += 16

            @blk.scalar
            def _(scalar):
                if not on(tag): return
                if extra_sc is not None:
                    extra_sc(scalar)
                for b_ in range(nblk):
                    for tt in range(n_ta_):
                        off, w = ta_list[tt]
                        for ic in range(IBC):
                            pi = pair_idx(b_, tt, ic)
                            scalar.wait_ge(c_mm.sem, 2 * pi + 1)
                            if pi >= 2:
                                scalar.wait_ge(c_hv.sem, pi - 1)
                            scalar.activation(sgb[pi % 2][:, 0:w], psgu[2 * (pi % 2)][:, 0:w], AF.Silu)
                            c_sg.inc(scalar.drain())

            @blk.tensor
            def _(tensor):
                if not on(tag): return
                for b_ in range(nblk):
                    sl = b_ % 2
                    tensor.wait_ge(c_wt[sl].sem, 32 * (b_ // 2 + 1))
                    for tt in range(n_ta_):
                        gt = b_ * n_ta_ + tt
                        off, w = ta_list[tt]
                        tensor.wait_ge(c_yb[gt % 2].sem, 16 * (gt // 2 + 1))
                        for ic in range(IBC):
                            pi = pair_idx(b_, tt, ic)
                            if pi >= 2:
                                tensor.wait_ge(c_hv.sem, pi - 1)
                            for wt, pso in ((wTg, 0), (wTu, 1)):
                                psd = psgu[2 * (pi % 2) + pso]
                                for dc in range(DC):
                                    mm = tensor.matmul(psd[:, 0:w],
                                                       wt[sl][:, dc, ic * 128:(ic + 1) * 128],
                                                       ybuf[gt % 2][:, dc, 0:w],
                                                       start=(dc == 0), stop=(dc == DC - 1))
                                c_mm.inc(mm)

            @blk.vector
            def _(vector):
                if not on(tag): return
                for b_ in range(nblk):
                    for tt in range(n_ta_):
                        off, w = ta_list[tt]
                        for ic in range(IBC):
                            pi = pair_idx(b_, tt, ic)
                            vector.wait_ge(c_mm.sem, 2 * pi + 2)
                            vector.wait_ge(c_sg.sem, pi + 1)
                            if pi >= 2:
                                vector.wait_ge(c_hw[pi % 2].sem, 16 * (pi // 2))
                            c_hv.inc(vector.tensor_tensor(hbuf[pi % 2][:, 0:w], sgb[pi % 2][:, 0:w],
                                                          psgu[2 * (pi % 2) + 1][:, 0:w], ALU.mult))

            @blk.gpsimd
            def _(g):
                if not on(tag): return
                if extra_gp is not None:
                    extra_gp(g)
                for b_ in range(nblk):
                    for tt in range(n_ta_):
                        off, w = ta_list[tt]
                        for ic in range(IBC):
                            pi = pair_idx(b_, tt, ic)
                            g.wait_ge(c_hv.sem, pi + 1)
                            g.dma_start(out=h_view(tt, w)[:, b_ * IBC + ic, :],
                                        in_=hbuf[pi % 2][:, 0:w]).then_inc(c_hw[pi % 2].sem, 16)
                            c_hw[pi % 2].n += 16
                g.wait_ge(c_hw[0].sem, c_hw[0].n)
                g.wait_ge(c_hw[1].sem, c_hw[1].n)
        sem_free(_m)

    sup = []
    c0 = 0
    while c0 < DC:
        sup.append((c0, min(3, DC - c0)))
        c0 += min(3, DC - c0)
    NOS = len(sup)
    cumoc = [0]
    for _, co_ in sup:
        cumoc.append(cumoc[-1] + co_)
    n_hd = 1
    NSLOT = 16

    def emit_ffn_B(tag, x_view, xout_view, ta_list, stats_rows):
        _m = sem_mark()
        n_ta_ = len(ta_list)
        TAW_ = max(w for _, w in ta_list)
        c_ht = [Ch(nc, f"{tag}_ht{j}") for j in range(2)]
        c_wds = [Ch(nc, f"{tag}_wd{j}") for j in range(NSLOT)]
        c_mmi = Ch(nc, f"{tag}_mmi")
        c_stt = Ch(nc, f"{tag}_stt")
        c_cast = Ch(nc, f"{tag}_cast")
        c_mmst = Ch(nc, f"{tag}_mmst")
        c_fin = Ch(nc, f"{tag}_fin")
        c_xs = [Ch(nc, f"{tag}_xs{j}") for j in range(2)]
        c_xw = [Ch(nc, f"{tag}_xw{j}") for j in range(2)]
        c_lnw = Ch(nc, f"{tag}_lnw")

        def mmi_after(tt, os_, ic):
            return (tt * NOS + os_) * n_ic + ic + 1

        def k_oc(tt, os_, j):
            return tt * DC + cumoc[os_] + j

        def sttc(gi):
            if gi < 0:
                return 0
            return (gi // NOS) * DC + cumoc[gi % NOS + 1]

        with ExitStack() as stk, nc.Block(tag) as blk:
            htile = [stk.enter_context(nc.sbuf_tensor(f"{tag}_ht{j}", [128, n_ic, TAW_], BF16)) for j in range(2)]
            wds = [stk.enter_context(nc.sbuf_tensor(f"{tag}_wd{j}", [128, 384], BF16)) for j in range(NSLOT)]
            xs = [stk.enter_context(nc.sbuf_tensor(f"{tag}_xs{j}", [128, 3, TAW_], F32)) for j in range(2)]
            xo = [stk.enter_context(nc.sbuf_tensor(f"{tag}_xo{j}", [128, 3, TAW_], F32)) for j in range(2)]
            if stats_rows is not None:
                xmbf = [stk.enter_context(nc.sbuf_tensor(f"{tag}_xmbf{j}", [128, TAW_], BF16)) for j in range(4)]
                sqbf = [stk.enter_context(nc.sbuf_tensor(f"{tag}_sqbf{j}", [128, TAW_], BF16)) for j in range(4)]
                srow = stk.enter_context(nc.sbuf_tensor(f"{tag}_srow", [1, 6 * TAW_], F32))
                psS = [stk.enter_context(nc.psum_tensor(f"{tag}_psS{j}", [1, TAW_], F32)) for j in range(2)]
            ps_o = [[stk.enter_context(nc.psum_tensor(f"{tag}_ps{j}_{k}", [128, TAW_], F32)) for k in range(3)]
                    for j in range(2)]

            sdTv = stage_dT.rearrange("(c p) d -> p c d", p=128)

            @blk.sync
            def _(sync):
                if not on(tag): return
                for tt in range(n_ta_):
                    for os_, (oc0, co) in enumerate(sup):
                        for ic in range(n_ic):
                            si = (tt * NOS + os_) * n_ic + ic
                            sl = si % NSLOT
                            if sl % 2 != 0:
                                continue
                            if si >= NSLOT:
                                sync.wait_ge(c_mmi.sem, si - (NSLOT - 2))
                            sync.dma_start(out=wds[sl][:, 0:co * 128],
                                           in_=sdTv[:, ic, oc0 * 128:(oc0 + co) * 128]
                                           ).then_inc(c_wds[sl].sem, 16)
                            c_wds[sl].n += 16

            @blk.scalar
            def _(scalar):
                if not on(tag): return

                def odd_strips(scalar, tt, os_):
                    oc0, co = sup[os_]
                    for ic in range(n_ic):
                        si = (tt * NOS + os_) * n_ic + ic
                        sl = si % NSLOT
                        if sl % 2 != 1:
                            continue
                        if si >= NSLOT:
                            scalar.wait_ge(c_mmi.sem, si - (NSLOT - 2))
                        scalar.dma_start(out=wds[sl][:, 0:co * 128],
                                         in_=sdTv[:, ic, oc0 * 128:(oc0 + co) * 128]
                                         ).then_inc(c_wds[sl].sem, 16)
                        c_wds[sl].n += 16

                def casts(scalar, tt, os_):
                    if stats_rows is None:
                        return
                    oc0, co = sup[os_]
                    off, w = ta_list[tt]
                    for j in range(co):
                        k = k_oc(tt, os_, j)
                        gi_ = tt * NOS + os_
                        scalar.wait_ge(c_stt.sem, k + 1)
                        if k >= 4:
                            scalar.wait_ge(c_mmst.sem, k - 3)
                        scalar.activation(xmbf[k % 4][:, 0:w], xo[gi_ % 2][:, j, 0:w], AF.Copy)
                        scalar.drain()
                        scalar.activation(sqbf[k % 4][:, 0:w], xo[gi_ % 2][:, j, 0:w], AF.Square)
                        c_cast.inc(scalar.drain())

                for tt in range(n_ta_):
                    off, w = ta_list[tt]
                    for os_ in range(NOS):
                        odd_strips(scalar, tt, os_)
                        if os_ >= 1:
                            casts(scalar, tt, os_ - 1)
                    casts(scalar, tt, NOS - 1)
                    if stats_rows is None:
                        continue
                    scalar.wait_ge(c_fin.sem, 4 * tt + 2)
                    scalar.activation(srow[:, 4 * TAW_:4 * TAW_ + w], srow[:, 2 * TAW_:2 * TAW_ + w], AF.Sqrt)
                    c_fin.inc(scalar.drain())

            @blk.tensor
            def _(tensor):
                if not on(tag): return
                for tt in range(n_ta_):
                    off, w = ta_list[tt]
                    tensor.wait_ge(c_ht[tt % 2].sem, 16 * n_hd * (tt // 2 + 1))

                    def stats_mms(tt_, po):
                        for j in range(sup[po][1]):
                            k = k_oc(tt_, po, j)
                            tensor.wait_ge(c_cast.sem, k + 1)
                            if k % DC == 0 and tt_ >= 1:
                                tensor.wait_ge(c_fin.sem, 4 * (tt_ - 1) + 1)
                            tensor.matmul(psS[0][0:1, 0:w], ones_bf[:, 0:1], xmbf[k % 4][:, 0:w],
                                          start=(k % DC == 0), stop=(k % DC == DC - 1))
                            mm = tensor.matmul(psS[1][0:1, 0:w], ones_bf[:, 0:1], sqbf[k % 4][:, 0:w],
                                               start=(k % DC == 0), stop=(k % DC == DC - 1))
                            c_mmst.inc(mm)

                    for os_, (oc0, co) in enumerate(sup):
                        gi = tt * NOS + os_
                        pso = ps_o[gi % 2]
                        if gi >= 2:
                            tensor.wait_ge(c_stt.sem, sttc(gi - 2))
                        for ic in range(n_ic):
                            si = gi * n_ic + ic
                            sl = si % NSLOT
                            tensor.wait_ge(c_wds[sl].sem, 16 * (si // NSLOT + 1))
                            for j in range(co):
                                mm = tensor.matmul(pso[j][:, 0:w],
                                                   wds[sl][:, j * 128:(j + 1) * 128],
                                                   htile[tt % 2][:, ic, 0:w],
                                                   start=(ic == 0), stop=(ic == n_ic - 1))
                            c_mmi.inc(mm)
                        if stats_rows is not None and os_ >= 1:
                            stats_mms(tt, os_ - 1)
                    if stats_rows is not None:
                        stats_mms(tt, NOS - 1)

            @blk.vector
            def _(vector):
                if not on(tag): return
                for tt in range(n_ta_):
                    off, w = ta_list[tt]
                    for os_, (oc0, co) in enumerate(sup):
                        gi = tt * NOS + os_
                        vector.wait_ge(c_mmi.sem, mmi_after(tt, os_, n_ic - 1))
                        vector.wait_ge(c_xs[gi % 2].sem, 16 * (gi // 2 + 1))
                        if gi >= 2:
                            vector.wait_ge(c_xw[gi % 2].sem, 16 * (gi // 2))
                            if stats_rows is not None:
                                vector.wait_ge(c_cast.sem, sttc(gi - 2))
                        for j in range(co):
                            c_stt.inc(vector.scalar_tensor_tensor(
                                xo[gi % 2][:, j, 0:w], ps_o[gi % 2][j][:, 0:w], 0.5,
                                xs[gi % 2][:, j, 0:w], ALU.mult, ALU.add))
                    if stats_rows is None:
                        continue
                    if tt >= 1:
                        vector.wait_ge(c_lnw.sem, 32 * tt)
                    vector.wait_ge(c_mmst.sem, DC * (tt + 1))
                    vector.tensor_scalar(srow[:, 0:w], psS[0][0:1, 0:w], 1.0 / D, None, ALU.mult)
                    vector.tensor_scalar(srow[:, TAW_:TAW_ + w], psS[1][0:1, 0:w], 1.0 / D, None, ALU.mult)
                    c_fin.inc(vector.drain())
                    vector.tensor_tensor(srow[:, 2 * TAW_:2 * TAW_ + w], srow[:, 0:w], srow[:, 0:w], ALU.mult)
                    vector.drain()
                    vector.tensor_tensor(srow[:, 2 * TAW_:2 * TAW_ + w], srow[:, TAW_:TAW_ + w],
                                         srow[:, 2 * TAW_:2 * TAW_ + w], ALU.subtract)
                    vector.drain()
                    vector.tensor_scalar(srow[:, 2 * TAW_:2 * TAW_ + w], srow[:, 2 * TAW_:2 * TAW_ + w],
                                         1e-5, None, ALU.add)
                    c_fin.inc(vector.drain())
                    vector.wait_ge(c_fin.sem, 4 * tt + 3)
                    vector.reciprocal(srow[:, 5 * TAW_:5 * TAW_ + w], srow[:, 4 * TAW_:4 * TAW_ + w])
                    c_fin.inc(vector.drain())

            @blk.gpsimd
            def _(g):
                if not on(tag): return
                for tt in range(n_ta_):
                    off, w = ta_list[tt]
                    if tt == 0:
                        for t2 in (0, 1):
                            if t2 >= n_ta_:
                                continue
                            o2, w2 = ta_list[t2]
                            g.dma_start(out=htile[t2 % 2][:, :, 0:w2],
                                        in_=h_view(t2, w2)).then_inc(c_ht[t2 % 2].sem, 16)
                            c_ht[t2 % 2].n += 16
                    for os_, (oc0, co) in enumerate(sup):
                        gi = tt * NOS + os_
                        if gi >= 2:
                            g.wait_ge(c_stt.sem, sttc(gi - 2))
                        g.dma_start(out=xs[gi % 2][:, 0:co, 0:w],
                                    in_=x_view[:, oc0:oc0 + co, off:off + w]).then_inc(c_xs[gi % 2].sem, 16)
                        c_xs[gi % 2].n += 16
                        g.wait_ge(c_stt.sem, sttc(gi))
                        g.dma_start(out=xout_view[:, oc0:oc0 + co, off:off + w],
                                    in_=xo[gi % 2][:, 0:co, 0:w]).then_inc(c_xw[gi % 2].sem, 16)
                        c_xw[gi % 2].n += 16
                    if stats_rows is not None:
                        mrow, rrow = stats_rows
                        g.wait_ge(c_fin.sem, 4 * (tt + 1))
                        g.dma_start(out=ln_rows[mrow:mrow + 1, off:off + w],
                                    in_=srow[:, 0:w]).then_inc(c_lnw.sem, 16)
                        g.dma_start(out=ln_rows[rrow:rrow + 1, off:off + w],
                                    in_=srow[:, 5 * TAW_:5 * TAW_ + w]).then_inc(c_lnw.sem, 16)
                        c_lnw.n += 32
                    if tt + 2 < n_ta_:
                        t2 = tt + 2
                        o2, w2 = ta_list[t2]
                        g.wait_ge(c_mmi.sem, mmi_after(tt, NOS - 1, n_ic - 1))
                        g.dma_start(out=htile[t2 % 2][:, :, 0:w2],
                                    in_=h_view(t2, w2)).then_inc(c_ht[t2 % 2].sem, 16)
                        c_ht[t2 % 2].n += 16
                g.wait_ge(c_lnw.sem, c_lnw.n)
                g.wait_ge(c_xw[0].sem, c_xw[0].n)
                g.wait_ge(c_xw[1].sem, c_xw[1].n)
        sem_free(_m)

    c_rr = Ch(nc, "s1r_rr")
    c_rw = Ch(nc, "s1r_rw")
    rp_iter = ([("c", oc, 0) for oc in range(24)]
               + [("q", mi, hc) for mi, m in enumerate((0, 1, 3)) for hc in range(8)]
               + [("v", h_, 0) for h_ in range(2)]
               + [("p", dc, 0) for dc in range(32)])
    rp_reads = [6 if m == "c" else 1 for (m, a, b2) in rp_iter]
    rp_cum = []
    _cc = 0
    for r_ in rp_reads:
        _cc += 16 * r_
        rp_cum.append(_cc)
    qkvst_v = qkvst.rearrange("m (c p) k -> p m c k", p=128)
    projst_v = projst.rearrange("(c p) k -> p c k", p=128)
    convst_v = convst.rearrange("w (c p) o -> p w c o", p=128)

    rp_writes = [32 if m == "c" else 16 for (m, a, b2) in rp_iter]
    rw_cum = []
    _cw = 0
    for r_ in rp_writes:
        _cw += r_
        rw_cum.append(_cw)

    def _rp_gp(g):
        for it_, (m, a, b2) in enumerate(rp_iter):
            if it_ >= 1:
                g.wait_ge(c_rw.sem, rw_cum[it_ - 1])
            if m == "c":
                rv = rp_buf[:, 0:12288].rearrange("p (h t c o) -> p h t c o", h=2, t=3, c=16)
                for tap in range(3):
                    for h_ in range(2):
                        g.dma_start(out=rv[:, h_, tap, :, :],
                                    in_=convst_v[:, tap, 16 * h_:16 * h_ + 16, a * 128:(a + 1) * 128]
                                    ).then_inc(c_rr.sem, 16)
            elif m == "q":
                g.dma_start(out=rp_buf[:, 0:1024],
                            in_=qkvst_v[:, (0, 1, 3)[a], :, b2 * 128:(b2 + 1) * 128]
                            ).then_inc(c_rr.sem, 16)
            elif m == "v":
                g.dma_start(out=rp_buf[:, 0:4096],
                            in_=qkvst_v[:, 2, :, a * 512:(a + 1) * 512]).then_inc(c_rr.sem, 16)
            else:
                g.dma_start(out=rp_buf[:, 0:1024],
                            in_=projst_v[:, :, a * 128:(a + 1) * 128]).then_inc(c_rr.sem, 16)
        c_rr.n = rp_cum[-1]

    def _rp_sc(scalar):
        for it_, (m, a, b2) in enumerate(rp_iter):
            scalar.wait_ge(c_rr.sem, rp_cum[it_])
            if m == "c":
                for half in range(2):
                    scalar.dma_start(out=convT[((a * 2 + half) * 128 * 6144):((a * 2 + half + 1) * 128 * 6144)]
                                     .rearrange("(p x) -> p x", p=128),
                                     in_=rp_buf[:, half * 6144:(half + 1) * 6144]
                                     ).then_inc(c_rw.sem, 16)
            elif m == "q":
                scalar.dma_start(out=qkvT_view(a * 8 + b2).rearrange("p w o -> p (w o)"),
                                 in_=rp_buf[:, 0:1024]).then_inc(c_rw.sem, 16)
            elif m == "v":
                scalar.dma_start(out=vT_view(a).rearrange("p w o -> p (w o)"),
                                 in_=rp_buf[:, 0:4096]).then_inc(c_rw.sem, 16)
            else:
                scalar.dma_start(out=projT_view(a).rearrange("p w o -> p (w o)"),
                                 in_=rp_buf[:, 0:1024]).then_inc(c_rw.sem, 16)
            c_rw.n = rw_cum[it_]
        scalar.wait_ge(c_rw.sem, c_rw.n)

    c_dtr = [Ch(nc, "s1t_dtr0"), Ch(nc, "s1t_dtr1")]
    c_dtw = [Ch(nc, "s1t_dtw0"), Ch(nc, "s1t_dtw1")]
    n_dtr = I_ // 128

    def _dt_sc(scalar):
        for j in range(n_dtr * 2):
            ic, dh = j // 2, j % 2
            if j >= 2:
                scalar.wait_ge(c_dtw[j % 2].sem, 16 * (j // 2))
            scalar.dma_start(out=dt_buf[j % 2][:, :],
                             in_=stage_d[dh * 2048:(dh + 1) * 2048, ic * 128:(ic + 1) * 128],
                             transpose=True).then_inc(c_dtr[j % 2].sem, 16)
            c_dtr[j % 2].n += 16

    def _dt_gp(g):
        sdv = stage_dT.rearrange("(c p) d -> p c d", p=128)
        for j in range(n_dtr * 2):
            ic, dh = j // 2, j % 2
            g.wait_ge(c_dtr[j % 2].sem, 16 * (j // 2 + 1))
            g.dma_start(out=sdv[:, ic, dh * 2048:(dh + 1) * 2048],
                        in_=dt_buf[j % 2][:, :]).then_inc(c_dtw[j % 2].sem, 16)
            c_dtw[j % 2].n += 16
        g.wait_ge(c_dtw[0].sem, c_dtw[0].n)
        g.wait_ge(c_dtw[1].sem, c_dtw[1].n)
        _rp_gp(g)

    def _dt_sc_all(scalar):
        _dt_sc(scalar)
        _rp_sc(scalar)

    with nc.sbuf_tensor("s1r_buf", [128, 12288], BF16) as rp_buf, \
         nc.sbuf_tensor("s1t_b0", [128, 2048], BF16) as _db0, \
         nc.sbuf_tensor("s1t_b1", [128, 2048], BF16) as _db1:
        dt_buf = [_db0, _db1]
        emit_ffn_A("s1b", y_view, cfg.ta, extra_gp=_dt_gp, extra_sc=_dt_sc_all)
    emit_ffn_B("s2", x_fm_v, xmid_v, cfg.ta, (0, 1))

    sup3 = []
    c0 = 0
    while c0 < T:
        sup3.append((c0, min(512, T - c0)))
        c0 += min(512, T - c0)
    NSUP = len(sup3)
    tb = []
    n_tb = math.ceil(T / 410)
    wb = math.ceil(T / n_tb)
    c0 = 0
    while c0 < T:
        ww = min(wb, T - c0)
        tb.append((c0, ww))
        c0 += ww

    _m_s3 = sem_mark()
    c3_pre = Ch(nc, "s3_pre")
    c3_lnb = [Ch(nc, f"s3_lnb{j}") for j in range(2)]
    c3_xm = [Ch(nc, f"s3_xm{j}") for j in range(2)]
    c3_y2 = Ch(nc, "s3_y2")
    c3_cw = [Ch(nc, f"s3_cw{j}") for j in range(2)]
    c3_cmh = Ch(nc, "s3_cmh")
    c3_qw = [Ch(nc, f"s3_qw{j}") for j in range(6)]
    c3_vw = [Ch(nc, f"s3_vw{j}") for j in range(2)]
    c3_ow = [Ch(nc, f"s3_ow{j}") for j in range(3)]
    c3_cev = Ch(nc, "s3_cev")
    c3_qk = Ch(nc, "s3_qk")
    c3_qkev = Ch(nc, "s3_qkev")
    c3_v = Ch(nc, "s3_v")
    c3_vev = Ch(nc, "s3_vev")
    c3_sc = Ch(nc, "s3_sc")
    c3_sm1 = Ch(nc, "s3_sm1")
    c3_exp = Ch(nc, "s3_exp")
    c3_p = Ch(nc, "s3_p")
    c3_pt = Ch(nc, "s3_pt")
    c3_ptev = Ch(nc, "s3_ptev")
    c3_o = Ch(nc, "s3_o")
    c3_oev = Ch(nc, "s3_oev")
    c3_wo = Ch(nc, "s3_wo")
    c3_woev = Ch(nc, "s3_woev")
    c3_pj = [Ch(nc, f"s3_pj{j}") for j in range(4)]
    c3_pjmm = Ch(nc, "s3_pjmm")
    c3_pjev = Ch(nc, "s3_pjev")
    c3_x2 = Ch(nc, "s3_x2")
    c3_xr = [Ch(nc, f"s3_xr{j}") for j in range(2)]
    c3_x2w = [Ch(nc, f"s3_x2w{j}") for j in range(2)]
    c3_c3 = Ch(nc, "s3_cast")
    c3_mmst = Ch(nc, "s3_mmst")
    c3_fin = Ch(nc, "s3_fin")
    c3_lnw = Ch(nc, "s3_lnw")

    with ExitStack() as stk, nc.Block("s3") as blk:
        qksl = [stk.enter_context(nc.sbuf_tensor(f"s3_qksl{j}", [128, 8, 128], BF16)) for j in range(6)]
        vsl = [stk.enter_context(nc.sbuf_tensor(f"s3_vsl{j}", [128, 8, 512], BF16)) for j in range(2)]
        osl = [stk.enter_context(nc.sbuf_tensor(f"s3_osl{j}", [128, 8, 128], BF16)) for j in range(3)]
        bqs = stk.enter_context(nc.sbuf_tensor("s3_bqs", [128, 8], F32))
        bks = stk.enter_context(nc.sbuf_tensor("s3_bks", [128, 8], F32))
        bos = stk.enter_context(nc.sbuf_tensor("s3_bos", [128, 8], F32))
        pbs = stk.enter_context(nc.sbuf_tensor("s3_pbs", [128, 32], F32))
        bcs = stk.enter_context(nc.sbuf_tensor("s3_bcs", [128, 24], F32))
        bvb = stk.enter_context(nc.sbuf_tensor("s3_bvb", [128, WQK], F32))
        m2b = stk.enter_context(nc.sbuf_tensor("s3_m2b", [128, 514], F32))
        r2b = stk.enter_context(nc.sbuf_tensor("s3_r2b", [128, 514], F32))
        xmc = [stk.enter_context(nc.sbuf_tensor(f"s3_xmc{j}", [128, 514], F32)) for j in range(2)]
        tmpc = [stk.enter_context(nc.sbuf_tensor(f"s3_tmpc{j}", [128, 514], F32)) for j in range(2)]
        y2su = stk.enter_context(nc.sbuf_tensor("s3_y2su", [128, DC, 514], BF16))
        cwt = [stk.enter_context(nc.sbuf_tensor(f"s3_cwt{j}", [128, 3, 16, 128], BF16)) for j in range(2)]
        csu = stk.enter_context(nc.sbuf_tensor("s3_csu", [128, 24, 512], BF16))
        qhsb = stk.enter_context(nc.sbuf_tensor("s3_qhsb", [128, 8, 128], BF16))
        khsb = stk.enter_context(nc.sbuf_tensor("s3_khsb", [128, 8, 128], BF16))
        vtsb = stk.enter_context(nc.sbuf_tensor("s3_vtsb", [128, WQK], BF16))
        ssb = [stk.enter_context(nc.sbuf_tensor(f"s3_ssb{j}", [128, 128], F32)) for j in range(2)]
        nmx = stk.enter_context(nc.sbuf_tensor("s3_nmx", [128, 8], F32))
        smr = stk.enter_context(nc.sbuf_tensor("s3_smr", [128, 8], F32))
        esb = [stk.enter_context(nc.sbuf_tensor(f"s3_esb{j}", [128, 128], F32)) for j in range(2)]
        pbf = [stk.enter_context(nc.sbuf_tensor(f"s3_pbf{j}", [128, 128], BF16)) for j in range(2)]
        ptb = [stk.enter_context(nc.sbuf_tensor(f"s3_ptb{j}", [128, 128], BF16)) for j in range(2)]
        osb = stk.enter_context(nc.sbuf_tensor("s3_osb", [128, 8, 128], BF16))
        apre = stk.enter_context(nc.sbuf_tensor("s3_apre", [128, 8, 512], BF16))
        pjs = [stk.enter_context(nc.sbuf_tensor(f"s3_pjs{j}", [128, 8, 128], BF16)) for j in range(4)]
        ptmp = [stk.enter_context(nc.sbuf_tensor(f"s3_ptmp{j}", [128, 512], F32)) for j in range(2)]
        xrs = [stk.enter_context(nc.sbuf_tensor(f"s3_xrs{j}", [128, 512], F32)) for j in range(2)]
        x2c = [stk.enter_context(nc.sbuf_tensor(f"s3_x2c{j}", [128, 512], F32)) for j in range(2)]
        x2bf = [stk.enter_context(nc.sbuf_tensor(f"s3_x2bf{j}", [128, 512], BF16)) for j in range(4)]
        sq3 = [stk.enter_context(nc.sbuf_tensor(f"s3_sq3{j}", [128, 512], BF16)) for j in range(4)]
        srow3 = stk.enter_context(nc.sbuf_tensor("s3_srow3", [1, 6 * 512], F32))
        pA = stk.enter_context(nc.psum_tensor("s3_pA", [128, 512], F32))
        pB = stk.enter_context(nc.psum_tensor("s3_pB", [128, 512], F32))
        p128 = [stk.enter_context(nc.psum_tensor(f"s3_p{j}", [128, 128], F32)) for j in range(4)]
        psS3 = [stk.enter_context(nc.psum_tensor(f"s3_psS{j}", [1, 512], F32)) for j in range(2)]

        nb_of = [sw // 128 for _, sw in sup3]
        cum_nb = [0]
        for x in nb_of:
            cum_nb.append(cum_nb[-1] + x)
        NBLK = cum_nb[-1]

        def blk_gidx(s, b):
            return cum_nb[s] + b

        @blk.sync
        def _(sync):
            if not on("s3"): return
            sync.dma_start(out=bqs[:, 0:8], in_=b_q.rearrange("(c p) -> p c", p=128), allow_slow_non_contiguous=True).then_inc(c3_pre.sem, 16)
            sync.dma_start(out=bks[:, 0:8], in_=b_k.rearrange("(c p) -> p c", p=128), allow_slow_non_contiguous=True).then_inc(c3_pre.sem, 16)
            sync.dma_start(out=bos[:, 0:8], in_=b_o.rearrange("(c p) -> p c", p=128), allow_slow_non_contiguous=True).then_inc(c3_pre.sem, 16)
            sync.dma_start(out=pbs[:, 0:32], in_=b_p.rearrange("(c p) -> p c", p=128), allow_slow_non_contiguous=True).then_inc(c3_pre.sem, 16)
            sync.dma_start(out=bcs[:, 0:24], in_=b_conv.rearrange("(c p) -> p c", p=128), allow_slow_non_contiguous=True).then_inc(c3_pre.sem, 16)
            sync.dma_start(out=bvb[:, :], in_=b_v.unsqueeze(0).broadcast_to([128, WQK])).then_inc(c3_pre.sem, 16)
            c3_pre.n += 96
            for s, (off_t, sw) in enumerate(sup3):
                for oc in range(24):
                    for half in range(2):
                        ci2 = (s * 24 + oc) * 2 + half
                        if ci2 >= 2:
                            sync.wait_ge(c3_cmh.sem, ci2 - 1)
                        sync.dma_start(out=cwt[ci2 % 2][:, :, :, :],
                                       in_=convT_view(oc, half)).then_inc(c3_cw[ci2 % 2].sem, 16)
                        c3_cw[ci2 % 2].n += 16
                for b in range(nb_of[s]):
                    gb = blk_gidx(s, b)
                    for m in range(2):
                        for hc in range(8):
                            qi = (gb * 2 + m) * 8 + hc
                            sl = qi % 6
                            if qi >= 6:
                                sync.wait_ge(c3_qk.sem, qi - 4)
                            sync.dma_start(out=qksl[sl][:, :, :],
                                           in_=qkvT_view(m * 8 + hc)).then_inc(c3_qw[sl].sem, 16)
                            c3_qw[sl].n += 16
                    for half in range(2):
                        vi = gb * 2 + half
                        if vi >= 2:
                            sync.wait_ge(c3_v.sem, vi - 1)
                        sync.dma_start(out=vsl[vi % 2][:, :, :],
                                       in_=vT_view(half)).then_inc(c3_vw[vi % 2].sem, 16)
                        c3_vw[vi % 2].n += 16
                    for oc8 in range(8):
                        wi = gb * 8 + oc8
                        sl = wi % 3
                        if wi >= 3:
                            sync.wait_ge(c3_wo.sem, wi - 2)
                        sync.dma_start(out=osl[sl][:, :, :],
                                       in_=qkvT_view(16 + oc8)).then_inc(c3_ow[sl].sem, 16)
                        c3_ow[sl].n += 16
                for dc in range(DC):
                    pi = s * DC + dc
                    sl = pi % 4
                    if pi >= 4:
                        sync.wait_ge(c3_pjmm.sem, pi - 3)
                    sync.dma_start(out=pjs[sl][:, :, :],
                                   in_=projT_view(dc)).then_inc(c3_pj[sl].sem, 16)
                    c3_pj[sl].n += 16

        @blk.gpsimd
        def _(g):
            if not on("s3"): return
            for s, (off_t, sw) in enumerate(sup3):
                if s >= 1:
                    g.wait_ge(c3_y2.sem, DC * s)
                g.dma_start(out=m2b[:, 0:sw + 2],
                            in_=ln_rows[0:1, off_t:off_t + sw + 2].broadcast_to([128, sw + 2])
                            ).then_inc(c3_lnb[s % 2].sem, 16)
                g.dma_start(out=r2b[:, 0:sw + 2],
                            in_=ln_rows[1:2, off_t:off_t + sw + 2].broadcast_to([128, sw + 2])
                            ).then_inc(c3_lnb[s % 2].sem, 16)
                c3_lnb[s % 2].n += 32
                for dc in range(DC):
                    ki = s * DC + dc
                    if ki >= 2:
                        g.wait_ge(c3_y2.sem, ki - 1)
                    g.dma_start(out=xmc[ki % 2][:, 0:sw + 2],
                                in_=xmid_v[:, dc, off_t:off_t + sw + 2]).then_inc(c3_xm[ki % 2].sem, 16)
                    c3_xm[ki % 2].n += 16
                for dc in range(DC):
                    pi = s * DC + dc
                    if pi >= 2:
                        g.wait_ge(c3_x2.sem, pi - 1)
                    g.dma_start(out=xrs[pi % 2][:, 0:sw],
                                in_=xmid_v[:, dc, off_t + 1:off_t + 1 + sw]).then_inc(c3_xr[pi % 2].sem, 16)
                    c3_xr[pi % 2].n += 16
                    g.wait_ge(c3_mmst.sem, pi + 1)
                    g.dma_start(out=x2_v[:, dc, off_t:off_t + sw],
                                in_=x2c[pi % 2][:, 0:sw]).then_inc(c3_x2w[pi % 2].sem, 16)
                    c3_x2w[pi % 2].n += 16
                g.wait_ge(c3_fin.sem, 4 * (s + 1))
                g.dma_start(out=ln_rows[2:3, off_t:off_t + sw], in_=srow3[:, 0:sw]).then_inc(c3_lnw.sem, 16)
                g.dma_start(out=ln_rows[3:4, off_t:off_t + sw],
                            in_=srow3[:, 5 * 512:5 * 512 + sw]).then_inc(c3_lnw.sem, 16)
                c3_lnw.n += 32
            g.wait_ge(c3_lnw.sem, c3_lnw.n)
            g.wait_ge(c3_x2w[0].sem, c3_x2w[0].n)
            g.wait_ge(c3_x2w[1].sem, c3_x2w[1].n)
            g.wait_ge(c3_cw[0].sem, c3_cw[0].n)
            g.wait_ge(c3_cw[1].sem, c3_cw[1].n)

        @blk.vector
        def _(vector):
            if not on("s3"): return
            vector.wait_ge(c3_pre.sem, c3_pre.n)
            vector.tensor_scalar(bqs[:, 0:8], bqs[:, 0:8], 0.125, None, ALU.mult)
            vector.drain()
            for s, (off_t, sw) in enumerate(sup3):
                vector.wait_ge(c3_lnb[s % 2].sem, 32 * (s // 2 + 1))
                if s >= 1:
                    vector.wait_ge(c3_cmh.sem, 48 * s)
                for dc in range(DC):
                    ki = s * DC + dc
                    vector.wait_ge(c3_xm[ki % 2].sem, 16 * (ki // 2 + 1))
                    vector.tensor_tensor(tmpc[ki % 2][:, 0:sw + 2], xmc[ki % 2][:, 0:sw + 2],
                                         m2b[:, 0:sw + 2], ALU.subtract)
                    vector.drain()
                    c3_y2.inc(vector.tensor_tensor(y2su[:, dc, 0:sw + 2], tmpc[ki % 2][:, 0:sw + 2],
                                                   r2b[:, 0:sw + 2], ALU.mult))
                for b in range(nb_of[s]):
                    gb = blk_gidx(s, b)
                    for half in range(2):
                        vector.wait_ge(c3_v.sem, 2 * gb + half + 1)
                        c3_vev.inc(vector.tensor_tensor(
                            vtsb[:, half * 512:(half + 1) * 512],
                            (pA if half == 0 else pB)[:, :],
                            bvb[:, half * 512:(half + 1) * 512], ALU.add))
                    for h in range(16):
                        hi_ = gb * 16 + h
                        vector.wait_ge(c3_sc.sem, hi_ + 1)
                        vector.tensor_tensor(ssb[h % 2][:, :], p128[h % 2][:, :], mask[:, :], ALU.add)
                        vector.drain()
                        vector.reduce_max(nmx[:, h % 8:h % 8 + 1], ssb[h % 2][:, :], axis=AX.X, negate=True)
                        c3_sm1.inc(vector.drain())
                        vector.wait_ge(c3_exp.sem, hi_ + 1)
                        vector.reduce_sum(smr[:, h % 8:h % 8 + 1], esb[h % 2][:, :], axis=AX.X)
                        vector.drain()
                        vector.reciprocal(smr[:, h % 8:h % 8 + 1], smr[:, h % 8:h % 8 + 1])
                        vector.drain()
                        if hi_ >= 2:
                            vector.wait_ge(c3_pt.sem, hi_ - 1)
                        c3_p.inc(vector.tensor_scalar(pbf[h % 2][:, :], esb[h % 2][:, :],
                                                      smr[:, h % 8:h % 8 + 1], None, ALU.mult))
                        vector.wait_ge(c3_pt.sem, hi_ + 1)
                        if hi_ >= 2:
                            vector.wait_ge(c3_o.sem, hi_ - 1)
                        c3_ptev.inc(vector.tensor_copy(ptb[h % 2][:, :], p128[2 + h % 2][:, :]))
                for dc in range(DC):
                    pi = s * DC + dc
                    vector.wait_ge(c3_pjev.sem, pi + 1)
                    vector.wait_ge(c3_xr[pi % 2].sem, 16 * (pi // 2 + 1))
                    if pi >= 2:
                        vector.wait_ge(c3_x2w[pi % 2].sem, 16 * (pi // 2))
                        vector.wait_ge(c3_c3.sem, pi - 1)
                    c3_x2.inc(vector.tensor_tensor(x2c[pi % 2][:, 0:sw], ptmp[pi % 2][:, 0:sw],
                                                   xrs[pi % 2][:, 0:sw], ALU.add))
                if s >= 1:
                    vector.wait_ge(c3_lnw.sem, 32 * s)
                vector.wait_ge(c3_mmst.sem, DC * (s + 1))
                vector.tensor_scalar(srow3[:, 0:sw], psS3[0][0:1, 0:sw], 1.0 / D, None, ALU.mult)
                vector.tensor_scalar(srow3[:, 512:512 + sw], psS3[1][0:1, 0:sw], 1.0 / D, None, ALU.mult)
                c3_fin.inc(vector.drain())
                vector.tensor_tensor(srow3[:, 2 * 512:2 * 512 + sw], srow3[:, 0:sw], srow3[:, 0:sw], ALU.mult)
                vector.drain()
                vector.tensor_tensor(srow3[:, 2 * 512:2 * 512 + sw], srow3[:, 512:512 + sw],
                                     srow3[:, 2 * 512:2 * 512 + sw], ALU.subtract)
                vector.drain()
                vector.tensor_scalar(srow3[:, 2 * 512:2 * 512 + sw], srow3[:, 2 * 512:2 * 512 + sw],
                                     1e-5, None, ALU.add)
                c3_fin.inc(vector.drain())
                vector.wait_ge(c3_fin.sem, 4 * s + 3)
                vector.reciprocal(srow3[:, 5 * 512:5 * 512 + sw], srow3[:, 4 * 512:4 * 512 + sw])
                c3_fin.inc(vector.drain())

        @blk.scalar
        def _(scalar):
            if not on("s3"): return
            for s, (off_t, sw) in enumerate(sup3):
                for oc in range(24):
                    ci = s * 24 + oc
                    scalar.wait_ge(c3_cmh.sem, 2 * (ci + 1))
                    scalar.activation(csu[:, oc, 0:sw], (pA if oc % 2 == 0 else pB)[:, 0:sw],
                                      AF.Identity, bias=bcs[:, oc:oc + 1])
                    c3_cev.inc(scalar.drain())
                for b in range(nb_of[s]):
                    gb = blk_gidx(s, b)
                    for m in range(2):
                        for hc in range(8):
                            qi = (gb * 2 + m) * 8 + hc
                            scalar.wait_ge(c3_qk.sem, qi + 1)
                            if m == 0:
                                scalar.activation(qhsb[:, hc, :], p128[hc % 2][:, :], AF.Identity,
                                                  bias=bqs[:, hc:hc + 1], scale=0.125)
                            else:
                                scalar.activation(khsb[:, hc, :], p128[2 + hc % 2][:, :], AF.Identity,
                                                  bias=bks[:, hc:hc + 1])
                            c3_qkev.inc(scalar.drain())
                    for h in range(16):
                        hi_ = gb * 16 + h
                        scalar.wait_ge(c3_sm1.sem, hi_ + 1)
                        scalar.activation(esb[h % 2][:, :], ssb[h % 2][:, :], AF.Exp,
                                          bias=nmx[:, h % 8:h % 8 + 1])
                        c3_exp.inc(scalar.drain())
                        if h % 2 == 1:
                            scalar.wait_ge(c3_o.sem, hi_ + 1)
                            scalar.activation(osb[:, h // 2, :], (pA if (h // 2) % 2 == 0 else pB)[:, 0:128],
                                              AF.Copy)
                            c3_oev.inc(scalar.drain())
                    for oc8 in range(8):
                        wi = gb * 8 + oc8
                        scalar.wait_ge(c3_wo.sem, wi + 1)
                        scalar.activation(apre[:, oc8, b * 128:(b + 1) * 128], p128[oc8 % 4][:, :],
                                          AF.Identity, bias=bos[:, oc8:oc8 + 1])
                        c3_woev.inc(scalar.drain())
                for dc in range(DC):
                    pi = s * DC + dc
                    scalar.wait_ge(c3_pjmm.sem, pi + 1)
                    if pi >= 2:
                        scalar.wait_ge(c3_x2.sem, pi - 1)
                    scalar.activation(ptmp[pi % 2][:, 0:sw], (pA if dc % 2 == 0 else pB)[:, 0:sw],
                                      AF.Identity, bias=pbs[:, dc:dc + 1])
                    c3_pjev.inc(scalar.drain())
                    scalar.wait_ge(c3_x2.sem, pi + 1)
                    if pi >= 4:
                        scalar.wait_ge(c3_mmst.sem, pi - 3)
                    scalar.activation(x2bf[pi % 4][:, 0:sw], x2c[pi % 2][:, 0:sw], AF.Copy)
                    scalar.drain()
                    scalar.activation(sq3[pi % 4][:, 0:sw], x2c[pi % 2][:, 0:sw], AF.Square)
                    c3_c3.inc(scalar.drain())
                scalar.wait_ge(c3_fin.sem, 4 * s + 2)
                scalar.activation(srow3[:, 4 * 512:4 * 512 + sw], srow3[:, 2 * 512:2 * 512 + sw], AF.Sqrt)
                c3_fin.inc(scalar.drain())

        @blk.tensor
        def _(tensor):
            if not on("s3"): return
            tensor.wait_ge(c3_pre.sem, c3_pre.n)
            for s, (off_t, sw) in enumerate(sup3):
                for oc in range(24):
                    ci = s * 24 + oc
                    if ci >= 2:
                        tensor.wait_ge(c3_cev.sem, ci - 1)
                    if s >= 1 and oc < 2:
                        tensor.wait_ge(c3_pjev.sem, DC * s)
                    if oc == 0:
                        tensor.wait_ge(c3_y2.sem, DC * (s + 1))
                    pd = pA if oc % 2 == 0 else pB
                    for half in range(2):
                        ci2 = ci * 2 + half
                        tensor.wait_ge(c3_cw[ci2 % 2].sem, 16 * (ci2 // 2 + 1))
                        for tap in range(3):
                            for dcl in range(16):
                                dc = 16 * half + dcl
                                mm = tensor.matmul(pd[:, 0:sw], cwt[ci2 % 2][:, tap, dcl, :],
                                                   y2su[:, dc, tap:tap + sw],
                                                   start=(half == 0 and tap == 0 and dcl == 0),
                                                   stop=(half == 1 and tap == 2 and dcl == 15))
                        c3_cmh.inc(mm)
                for b in range(nb_of[s]):
                    gb = blk_gidx(s, b)
                    tensor.wait_ge(c3_cev.sem, 24 * (s + 1))
                    if gb >= 1:
                        tensor.wait_ge(c3_woev.sem, 8 * gb)
                    for m in range(2):
                        base = 0 if m == 0 else 8
                        for hc in range(8):
                            qi = (gb * 2 + m) * 8 + hc
                            pd = p128[hc % 2] if m == 0 else p128[2 + hc % 2]
                            tensor.wait_ge(c3_qw[qi % 6].sem, 16 * (qi // 6 + 1))
                            if qi >= 2:
                                tensor.wait_ge(c3_qkev.sem, qi - 1)
                            for wc in range(8):
                                mm = tensor.matmul(pd[:, :], qksl[qi % 6][:, wc, :],
                                                   csu[:, base + wc, b * 128:(b + 1) * 128],
                                                   start=(wc == 0), stop=(wc == 7))
                            c3_qk.inc(mm)
                    for half in range(2):
                        vi = gb * 2 + half
                        pd = pA if half == 0 else pB
                        if gb >= 1:
                            tensor.wait_ge(c3_vev.sem, vi - 1)
                            tensor.wait_ge(c3_oev.sem, 8 * gb)
                        tensor.wait_ge(c3_vw[vi % 2].sem, 16 * (vi // 2 + 1))
                        for wc in range(8):
                            mm = tensor.matmul(pd[:, :], csu[:, 16 + wc, b * 128:(b + 1) * 128],
                                               vsl[vi % 2][:, wc, :],
                                               start=(wc == 0), stop=(wc == 7))
                        c3_v.inc(mm)
                    tensor.wait_ge(c3_qkev.sem, (gb * 2 + 2) * 8)
                    tensor.wait_ge(c3_vev.sem, 2 * gb + 2)
                    for h in range(16):
                        hi_ = gb * 16 + h
                        hp = (h % 2) * 64
                        if hi_ >= 1:
                            tensor.wait_ge(c3_sm1.sem, hi_)
                        c3_sc.inc(tensor.matmul(p128[h % 2][:, :],
                                                qhsb[hp:hp + 64, h // 2, :],
                                                khsb[hp:hp + 64, h // 2, :],
                                                start=True, stop=True))
                        tensor.wait_ge(c3_p.sem, hi_ + 1)
                        if hi_ >= 1:
                            tensor.wait_ge(c3_ptev.sem, hi_)
                        c3_pt.inc(tensor.matmul(p128[2 + h % 2][:, :], pbf[h % 2][:, :],
                                                ident_bf[:, :], start=True, stop=True))
                        tensor.wait_ge(c3_ptev.sem, hi_ + 1)
                        pd = pA if (h // 2) % 2 == 0 else pB
                        if hi_ >= 4 and h % 2 == 0:
                            tensor.wait_ge(c3_oev.sem, hi_ // 2 - 1)
                        c3_o.inc(tensor.matmul(pd[hp:hp + 64, 0:128],
                                               vtsb[:, h * 64:(h + 1) * 64], ptb[h % 2][:, :],
                                               start=True, stop=True))
                    for oc8 in range(8):
                        wi = gb * 8 + oc8
                        tensor.wait_ge(c3_oev.sem, gb * 8 + 8)
                        if wi >= 4:
                            tensor.wait_ge(c3_woev.sem, wi - 3)
                        tensor.wait_ge(c3_ow[wi % 3].sem, 16 * (wi // 3 + 1))
                        for hc in range(8):
                            mm = tensor.matmul(p128[oc8 % 4][:, :], osl[wi % 3][:, hc, :],
                                               osb[:, hc, :],
                                               start=(hc == 0), stop=(hc == 7))
                        c3_wo.inc(mm)
                def stats3_mm(dc_):
                    pi_ = s * DC + dc_
                    tensor.wait_ge(c3_c3.sem, pi_ + 1)
                    if dc_ == 0 and s >= 1:
                        tensor.wait_ge(c3_fin.sem, 4 * (s - 1) + 1)
                    tensor.matmul(psS3[0][0:1, 0:sw], ones_bf[:, 0:1], x2bf[pi_ % 4][:, 0:sw],
                                  start=(dc_ == 0), stop=(dc_ == DC - 1))
                    mm_ = tensor.matmul(psS3[1][0:1, 0:sw], ones_bf[:, 0:1], sq3[pi_ % 4][:, 0:sw],
                                        start=(dc_ == 0), stop=(dc_ == DC - 1))
                    c3_mmst.inc(mm_)

                for dc in range(DC):
                    pi = s * DC + dc
                    sl = pi % 4
                    tensor.wait_ge(c3_pj[sl].sem, 16 * (pi // 4 + 1))
                    tensor.wait_ge(c3_woev.sem, (cum_nb[s + 1]) * 8)
                    if pi >= 2:
                        tensor.wait_ge(c3_pjev.sem, pi - 1)
                    pd = pA if dc % 2 == 0 else pB
                    for wc in range(8):
                        mm = tensor.matmul(pd[:, 0:sw], pjs[sl][:, wc, :], apre[:, wc, 0:sw],
                                           start=(wc == 0), stop=(wc == 7))
                    c3_pjmm.inc(mm)
                    if dc >= 2:
                        stats3_mm(dc - 2)
                stats3_mm(DC - 2)
                stats3_mm(DC - 1)

    sem_free(_m_s3)

    _m_s3b = sem_mark()
    c3b_ln = Ch(nc, "s3b_ln")
    c3b_x = [Ch(nc, f"s3b_x{j}") for j in range(2)]
    c3b_y = Ch(nc, "s3b_y")
    c3b_w = [Ch(nc, f"s3b_w{j}") for j in range(2)]
    _p3b = []
    for _s, (_o, _sw) in enumerate(sup3):
        _np_ = sum(1 for (toff, tw) in tb if max(_o, toff) < min(_o + _sw, toff + tw))
        _p3b.append(16 * _np_)
    cum3b = []
    _c3 = [0, 0]
    for _s in range(len(sup3)):
        for _dc in range(DC):
            _ki = _s * DC + _dc
            _c3[_ki % 2] += _p3b[_s]
            cum3b.append(_c3[_ki % 2])
    with ExitStack() as stk, nc.Block("s3b") as blk:
        m3b = stk.enter_context(nc.sbuf_tensor("s3b_m3b", [128, 512], F32))
        r3b = stk.enter_context(nc.sbuf_tensor("s3b_r3b", [128, 512], F32))
        xc = [stk.enter_context(nc.sbuf_tensor(f"s3b_xc{j}", [128, 512], F32)) for j in range(2)]
        tc_ = [stk.enter_context(nc.sbuf_tensor(f"s3b_tc{j}", [128, 512], F32)) for j in range(2)]
        yc = [stk.enter_context(nc.sbuf_tensor(f"s3b_yc{j}", [128, 512], BF16)) for j in range(2)]

        @blk.gpsimd
        def _(g):
            if not on("s3b"): return
            for s, (off_t, sw) in enumerate(sup3):
                if s >= 1:
                    g.wait_ge(c3b_y.sem, DC * s)
                g.dma_start(out=m3b[:, 0:sw],
                            in_=ln_rows[2:3, off_t:off_t + sw].broadcast_to([128, sw])).then_inc(c3b_ln.sem, 16)
                g.dma_start(out=r3b[:, 0:sw],
                            in_=ln_rows[3:4, off_t:off_t + sw].broadcast_to([128, sw])).then_inc(c3b_ln.sem, 16)
                c3b_ln.n += 32
                for dc in range(DC):
                    ki = s * DC + dc
                    if ki >= 2:
                        g.wait_ge(c3b_y.sem, ki - 1)
                    g.dma_start(out=xc[ki % 2][:, 0:sw],
                                in_=x2_v[:, dc, off_t:off_t + sw]).then_inc(c3b_x[ki % 2].sem, 16)
                    c3b_x[ki % 2].n += 16

        @blk.vector
        def _(vector):
            if not on("s3b"): return
            for s, (off_t, sw) in enumerate(sup3):
                vector.wait_ge(c3b_ln.sem, 32 * (s + 1))
                for dc in range(DC):
                    ki = s * DC + dc
                    vector.wait_ge(c3b_x[ki % 2].sem, 16 * (ki // 2 + 1))
                    if ki >= 2:
                        vector.wait_ge(c3b_w[ki % 2].sem, cum3b[ki - 2])
                    vector.tensor_tensor(tc_[ki % 2][:, 0:sw], xc[ki % 2][:, 0:sw],
                                         m3b[:, 0:sw], ALU.subtract)
                    vector.drain()
                    c3b_y.inc(vector.tensor_tensor(yc[ki % 2][:, 0:sw], tc_[ki % 2][:, 0:sw],
                                                   r3b[:, 0:sw], ALU.mult))

        @blk.scalar
        def _(scalar):
            if not on("s3b"): return
            for s, (off_t, sw) in enumerate(sup3):
                for dc in range(DC):
                    ki = s * DC + dc
                    scalar.wait_ge(c3b_y.sem, ki + 1)
                    for tt, (toff, tw) in enumerate(tb):
                        lo = max(off_t, toff)
                        hi = min(off_t + sw, toff + tw)
                        if lo >= hi:
                            continue
                        scalar.dma_start(out=y_view(tt, tw)[:, dc, lo - toff:hi - toff],
                                         in_=yc[ki % 2][:, lo - off_t:hi - off_t]
                                         ).then_inc(c3b_w[ki % 2].sem, 16)
                    c3b_w[ki % 2].n = cum3b[ki]
            scalar.wait_ge(c3b_w[0].sem, c3b_w[0].n)
            scalar.wait_ge(c3b_w[1].sem, c3b_w[1].n)

    sem_free(_m_s3b)

    emit_ffn_A("s4a", y_view, tb)
    emit_ffn_B("s4b", x2_v, x3_v, tb, None)

    ns5 = T // 128
    _m_s5 = sem_mark()
    c5_in = [Ch(nc, f"s5_in{j}") for j in range(2)]
    c5_tr = Ch(nc, "s5_tr")
    c5_ev = Ch(nc, "s5_ev")
    c5_w = [Ch(nc, f"s5_w{j}") for j in range(2)]
    with ExitStack() as stk, nc.Block("s5") as blk:
        xin5 = [stk.enter_context(nc.sbuf_tensor(f"s5_xin{j}", [128, DC, 128], F32)) for j in range(2)]
        ob = [stk.enter_context(nc.sbuf_tensor(f"s5_ob{j}", [128, D], F32)) for j in range(2)]
        ps5 = [stk.enter_context(nc.psum_tensor(f"s5_ps{j}", [128, 512], F32)) for j in range(2)]

        @blk.sync
        def _(sync):
            if not on("s5"): return
            for t in range(ns5):
                b = t % 2
                if t >= 2:
                    sync.wait_ge(c5_tr.sem, 8 * (t - 1))
                sync.dma_start(out=xin5[b][:, :, :],
                               in_=x3_v[:, :, t * 128:(t + 1) * 128]).then_inc(c5_in[b].sem, 16)
                c5_in[b].n += 16

        @blk.tensor
        def _(tensor):
            if not on("s5"): return
            for t in range(ns5):
                b = t % 2
                tensor.wait_ge(c5_in[b].sem, 16 * (t // 2 + 1))
                for r in range(8):
                    R = 8 * t + r
                    if R >= 2:
                        tensor.wait_ge(c5_ev.sem, R - 1)
                    for k in range(4):
                        dc = r * 4 + k
                        mm = tensor.matmul(ps5[R % 2][:, k * 128:(k + 1) * 128],
                                           xin5[b][:, dc, :], ident_f[:, :],
                                           start=True, stop=True)
                    c5_tr.inc(mm)

        @blk.vector
        def _(vector):
            if not on("s5"): return
            for t in range(ns5):
                b = t % 2
                for r in range(8):
                    R = 8 * t + r
                    vector.wait_ge(c5_tr.sem, R + 1)
                    if t >= 2 and r == 0:
                        vector.wait_ge(c5_w[b].sem, 16 * (t // 2))
                    c5_ev.inc(vector.tensor_copy(ob[b][:, r * 512:(r + 1) * 512], ps5[R % 2][:, :]))

        @blk.gpsimd
        def _(g):
            if not on("s5"): return
            for t in range(ns5):
                b = t % 2
                g.wait_ge(c5_ev.sem, 8 * (t + 1))
                g.dma_start(out=out[t * 128:(t + 1) * 128, :], in_=ob[b][:, :]).then_inc(c5_w[b].sem, 16)
                c5_w[b].n += 16
            g.wait_ge(c5_w[0].sem, c5_w[0].n)
            g.wait_ge(c5_w[1].sem, c5_w[1].n)

    sem_free(_m_s5)

    c_dbg = Ch(nc, "dbg")
    with nc.Block("finish") as blk:
        @blk.sync
        def _(sync):
            for name, (srct, o, shape) in dbg.items():
                flat = " ".join(chr(97 + i) for i in range(len(shape)))
                pat = f"{flat} -> ({flat})"
                sync.dma_start(out=o.rearrange(pat), in_=srct.rearrange(pat)).then_inc(c_dbg.sem, 16)
                c_dbg.n += 16
            sync.wait_ge(c_dbg.sem, c_dbg.n)

    return nc, dbg


def core_inputs(inp, core, cfg: Cfg):
    b, half = core // 2, core % 2
    x_full = inp["L0_attn_inputs"][b]
    T = cfg.T
    own = np.ascontiguousarray(x_full[half * 2048:half * 2048 + T])
    hl = (x_full[half * 2048 - 1:half * 2048] if half == 1
          else np.zeros((1, D), np.float32))
    nxt = half * 2048 + T
    hr = (x_full[nxt:nxt + 1] if (half == 0 and nxt < x_full.shape[0])
          else np.zeros((1, D), np.float32))
    m = {
        "x_own": own, "x_hl": np.ascontiguousarray(hl), "x_hr": np.ascontiguousarray(hr),
        "w_conv": inp["L0_attn_conv_w"], "b_conv": inp["L0_attn_conv_b"],
        "w_q": inp["L0_attn_wq"].reshape(WQK, NHEAD * DH),
        "b_q": inp["L0_attn_bq"].reshape(NHEAD * DH),
        "w_k": inp["L0_attn_wk"].reshape(WQK, NHEAD * DH),
        "b_k": inp["L0_attn_bk"].reshape(NHEAD * DH),
        "w_v": inp["L0_attn_wv"].reshape(WQK, NHEAD * DH),
        "b_v": inp["L0_attn_bv"].reshape(NHEAD * DH),
        "w_o": inp["L0_attn_wo"].reshape(NHEAD * DH, WQK),
        "b_o": inp["L0_attn_bo"],
        "w_p": inp["L0_attn_proj_w"], "b_p": inp["L0_attn_proj_b"],
        "gq": inp["L0_ffn_gate_Wq"][:cfg.I], "uq": inp["L0_ffn_up_Wq"][:cfg.I],
        "dq": inp["L0_ffn_down_Wq"][:, :cfg.I // 2],
        "gsc": inp["L0_ffn_gate_scale"][:cfg.I], "gzr": inp["L0_ffn_gate_zero"][:cfg.I],
        "usc": inp["L0_ffn_up_scale"][:cfg.I], "uzr": inp["L0_ffn_up_zero"][:cfg.I],
        "dsc": inp["L0_ffn_down_scale"][:, :cfg.I // 64],
        "dzr": inp["L0_ffn_down_zero"][:, :cfg.I // 64],
    }
    return {k: np.ascontiguousarray(v) for k, v in m.items()}


from concourse.bass_utils import run_bass_kernel_spmd


def kernel(**inputs):
    cfg = Cfg()
    nc, _ = build_program(cfg)
    in_maps = [core_inputs(inputs, c, cfg) for c in range(8)]
    res = run_bass_kernel_spmd(nc, in_maps, core_ids=list(range(8)))
    out = np.empty((4, 4096, 4096), np.float32)
    for c in range(8):
        b, half = c // 2, c % 2
        out[b, half * 2048:(half + 1) * 2048] = np.asarray(res.results[c]["out"])
    return out
